# version 03
# baseline (speedup 1.0000x reference)
import contextlib
import os
import numpy as np

import concourse.bass as bass
import concourse.mybir as mybir
from concourse.bass_utils import run_bass_kernel_spmd
from concourse.alu_op_type import AluOpType as ALU

F32 = mybir.dt.float32
BF16 = mybir.dt.bfloat16
I32 = mybir.dt.int32
U8 = mybir.dt.uint8
AF = mybir.ActivationFunctionType
AX = mybir.AxisListType

ENGINES = ("pe", "act", "dve", "pool", "sp")
DMA_QUEUES = ("sp", "act", "pool")
DMA_SLOTS = {"sp": 12, "act": 6, "pool": 12}
NEG = -1.0e30


class Tile:
    __slots__ = ("h", "key", "shape", "dtype", "space")

    def __init__(self, h, key, shape, dtype, space):
        self.h = h
        self.key = key
        self.shape = tuple(shape)
        self.dtype = dtype
        self.space = space

    def __getitem__(self, sl):
        return self.h[sl]

    @property
    def ap(self):
        return self.h[:]


def _v(x):
    if isinstance(x, Tile):
        return x.h[:], x.key
    t, sl = x
    if isinstance(t, Tile):
        return t.h[sl], t.key
    return t, sl


class _Op:
    __slots__ = ("idx", "eng", "emit", "waits", "dma_q", "dma_ord", "tag", "mnum")

    def __init__(self, idx, eng, emit, waits, dma_q, dma_ord, tag):
        self.idx = idx
        self.eng = eng
        self.emit = emit
        self.waits = waits
        self.dma_q = dma_q
        self.dma_ord = dma_ord
        self.tag = tag
        self.mnum = None


class P:
    def __init__(self, name="mk"):
        self.nc = bass.Bass("TRN2", target_bir_lowering=False)
        self.ctx = contextlib.ExitStack()
        self.ops = []
        self.eng_ops = {e: [] for e in ENGINES}
        self.bufs = {}
        self._fence = []
        self.dma_n = {q: 0 for q in DMA_QUEUES}
        self._uid = 0
        self.finalized = False
        self._scope = None
        self._scope_tiles = None

    def _name(self, base):
        self._uid += 1
        return f"{base}_{self._uid}"

    def sbuf(self, name, shape, dtype):
        st = self._scope if self._scope is not None else self.ctx
        h = st.enter_context(self.nc.sbuf_tensor(self._name(name), list(shape), dtype))
        t = Tile(h, h.name, shape, dtype, "sbuf")
        self._track_new(t)
        return t

    def psum(self, name, shape, dtype=F32):
        st = self._scope if self._scope is not None else self.ctx
        h = st.enter_context(self.nc.psum_tensor(self._name(name), list(shape), dtype))
        t = Tile(h, h.name, shape, dtype, "psum")
        self._track_new(t)
        return t

    def _track_new(self, t):
        b = self._buf(t.key)
        if self._fence:
            b["fence"] = list(self._fence)
        if self._scope_tiles is not None:
            self._scope_tiles.append(t.key)

    @contextlib.contextmanager
    def scope(self):
        import contextlib as _cl
        prev_scope, prev_tiles = self._scope, self._scope_tiles
        st = _cl.ExitStack()
        self._scope, self._scope_tiles = st, []
        try:
            yield
        finally:
            deps = []
            for k in self._scope_tiles:
                b = self._buf(k)
                if b["w"] is not None:
                    deps.append(b["w"])
                deps.extend(b["r"])
            st.close()
            self._scope, self._scope_tiles = prev_scope, prev_tiles
            fd = {}
            for i in sorted(set(self._fence + deps)):
                o = self.ops[i]
                k = (o.eng, o.dma_q, o.dma_ord % DMA_SLOTS[o.dma_q] if o.dma_q else 0)
                fd[k] = i
            self._fence = sorted(fd.values())

    def dram_in(self, name, shape, dtype=F32):
        ap = self.nc.dram_tensor(name, list(shape), dtype, kind="ExternalInput").ap()
        return Tile(ap, name, shape, dtype, "dram")

    def dram_out(self, name, shape, dtype=F32):
        ap = self.nc.dram_tensor(name, list(shape), dtype, kind="ExternalOutput").ap()
        return Tile(ap, name, shape, dtype, "dram")

    def dram_tmp(self, name, shape, dtype=F32):
        nm = self._name(name)
        ap = self.nc.dram_tensor(nm, list(shape), dtype, kind="Internal").ap()
        return Tile(ap, nm, shape, dtype, "dram")

    def _buf(self, key):
        b = self.bufs.get(key)
        if b is None:
            b = {"w": None, "r": []}
            self.bufs[key] = b
        return b

    def _plan(self, eng, emit, r, w, tag="", dma_q=None):
        idx = len(self.ops)
        waits = set()
        for k in r:
            b = self._buf(k)
            if b["w"] is not None:
                waits.add(b["w"])
        for k in w:
            b = self._buf(k)
            if b["w"] is not None:
                waits.add(b["w"])
            waits.update(b["r"])
            f = b.pop("fence", None)
            if f:
                waits.update(f)
        best = {}
        for pidx in waits:
            po = self.ops[pidx]
            if po.eng == "pe" and eng == "pe":
                continue
            k = (po.eng, po.dma_q, po.dma_ord % DMA_SLOTS[po.dma_q] if po.dma_q else 0)
            if best.get(k, -1) < pidx:
                best[k] = pidx
        wl = list(best.values())
        dma_ord = None
        if dma_q is not None:
            dma_ord = self.dma_n[dma_q]
            self.dma_n[dma_q] += 1
        op = _Op(idx, eng, emit, wl, dma_q, dma_ord, tag)
        self.ops.append(op)
        self.eng_ops[eng].append(idx)
        for k in r:
            self._buf(k)["r"].append(idx)
        for k in w:
            b = self._buf(k)
            b["w"] = idx
            b["r"] = []
        return op

    def mm(self, out, lhsT, rhs, start=True, stop=True, tag="mm"):
        oa, ok = _v(out)
        la, lk = _v(lhsT)
        ra, rk = _v(rhs)
        self._plan("pe", lambda e: e.matmul(oa, la, ra, start=start, stop=stop),
                   [lk, rk], [ok], tag)

    def transpose(self, out, in_, ident, tag="tr"):
        oa, ok = _v(out)
        ia, ik = _v(in_)
        da, dk = _v(ident)
        self._plan("pe", lambda e: e.transpose(oa, ia, da), [ik, dk], [ok], tag)

    def act(self, out, in_, func, bias=0.0, scale=1.0, accum=None, tag="act"):
        oa, ok = _v(out)
        ia, ik = _v(in_)
        r = [ik]
        w = [ok]
        if not isinstance(bias, (int, float)):
            ba, bk = _v(bias)
            r.append(bk)
            bias = ba
        if not isinstance(scale, (int, float)):
            sa, sk = _v(scale)
            r.append(sk)
            scale = sa
        kw = {}
        if accum is not None:
            aa, ak = _v(accum)
            w.append(ak)
            kw["accum_out"] = aa
        b, s = bias, scale
        self._plan("act", lambda e: e.activation(oa, ia, func, bias=b, scale=s, **kw), r, w, tag)

    def scopy(self, out, in_, tag="scopy"):
        oa, ok = _v(out)
        ia, ik = _v(in_)
        self._plan("act", lambda e: e.copy(oa, ia), [ik], [ok], tag)

    def _tt(self, eng, out, a, b, op, tag):
        oa, ok = _v(out)
        aa, ak = _v(a)
        ba, bk = _v(b)
        self._plan(eng, lambda e: e.tensor_tensor(oa, aa, ba, op), [ak, bk], [ok], tag)

    def vtt(self, out, a, b, op, tag="vtt"):
        self._tt("dve", out, a, b, op, tag)

    def ptt(self, out, a, b, op, tag="ptt"):
        self._tt("pool", out, a, b, op, tag)

    def _ts(self, eng, out, in_, s1, s2, op0, op1, accum, tag):
        oa, ok = _v(out)
        ia, ik = _v(in_)
        r = [ik]
        w = [ok]
        if not isinstance(s1, (int, float, type(None))):
            a1, k1 = _v(s1)
            r.append(k1)
            s1 = a1
        if not isinstance(s2, (int, float, type(None))):
            a2, k2 = _v(s2)
            r.append(k2)
            s2 = a2
        kw = {}
        if accum is not None:
            aa, ak = _v(accum)
            w.append(ak)
            kw["accum_out"] = aa
        self._plan(eng, lambda e: e.tensor_scalar(oa, ia, s1, s2, op0, op1, **kw), r, w, tag)

    def vts(self, out, in_, s1, op0, s2=None, op1=ALU.bypass, accum=None, tag="vts"):
        self._ts("dve", out, in_, s1, s2, op0, op1, accum, tag)

    def pts(self, out, in_, s1, op0, s2=None, op1=ALU.bypass, accum=None, tag="pts"):
        self._ts("pool", out, in_, s1, s2, op0, op1, accum, tag)

    def vstt(self, out, in0, scalar, in1, op0, op1, accum=None, tag="vstt"):
        oa, ok = _v(out)
        a0, k0 = _v(in0)
        a1, k1 = _v(in1)
        r = [k0, k1]
        w = [ok]
        if not isinstance(scalar, (int, float)):
            sa, sk = _v(scalar)
            r.append(sk)
            scalar = sa
        kw = {}
        if accum is not None:
            aa, ak = _v(accum)
            w.append(ak)
            kw["accum_out"] = aa
        s = scalar
        self._plan("dve", lambda e: e.scalar_tensor_tensor(oa, a0, s, a1, op0, op1, **kw), r, w, tag)

    def vred(self, out, in_, op=ALU.add, negate=None, tag="vred"):
        oa, ok = _v(out)
        ia, ik = _v(in_)
        self._plan("dve", lambda e: e.tensor_reduce(oa, ia, AX.X, op, negate=negate), [ik], [ok], tag)

    def vscan(self, out, d0, d1, initial, op0, op1, tag="vscan"):
        oa, ok = _v(out)
        a0, k0 = _v(d0)
        a1, k1 = _v(d1)
        r = [k0, k1]
        if not isinstance(initial, (int, float)):
            ia, ik2 = _v(initial)
            r.append(ik2)
            initial = ia
        init = initial
        self._plan("dve", lambda e: e.tensor_tensor_scan(oa, a0, a1, init, op0, op1), r, [ok], tag)

    def vrecip(self, out, in_, tag="vrecip"):
        oa, ok = _v(out)
        ia, ik = _v(in_)
        self._plan("dve", lambda e: e.reciprocal(oa, ia), [ik], [ok], tag)

    def vcopy(self, out, in_, tag="vcopy"):
        oa, ok = _v(out)
        ia, ik = _v(in_)
        self._plan("dve", lambda e: e.tensor_copy(oa, ia), [ik], [ok], tag)

    def pcopy(self, out, in_, tag="pcopy"):
        oa, ok = _v(out)
        ia, ik = _v(in_)
        self._plan("pool", lambda e: e.tensor_copy(oa, ia), [ik], [ok], tag)

    def vmemset(self, out, val, tag="vmemset"):
        oa, ok = _v(out)
        self._plan("dve", lambda e: e.memset(oa, val), [], [ok], tag)

    def pmemset(self, out, val, tag="pmemset"):
        oa, ok = _v(out)
        self._plan("pool", lambda e: e.memset(oa, val), [], [ok], tag)

    def vpred(self, out, mask, data, tag="vpred"):
        oa, ok = _v(out)
        ma, mk = _v(mask)
        da, dk = _v(data)
        self._plan("dve", lambda e: e.copy_predicated(oa, ma, da), [mk, dk, ok], [ok], tag)

    def iota(self, out, pattern, base=0, channel_multiplier=0, tag="iota"):
        oa, ok = _v(out)
        self._plan("pool", lambda e: e.iota(oa, pattern=pattern, base=base,
                                            channel_multiplier=channel_multiplier), [], [ok], tag)

    def dma(self, q, out, in_, tag="dma"):
        oa, ok = _v(out)
        ia, ik = _v(in_)
        self._plan(q, lambda e: e.dma_start(out=oa, in_=ia), [ik], [ok], tag, dma_q=q)

    def raw(self, eng, emit, r, w, tag="raw"):
        self._plan(eng, emit, list(r), list(w), tag)

    def collective(self, kind, op, replica_groups, ins, outs, r, w, tag="cc"):
        def emit(e):
            return e.collective_compute(kind, op, replica_groups, ins, outs)
        self._plan("pool", emit, list(r), list(w), tag, dma_q="pool")

    def build(self):
        assert not self.finalized
        self.finalized = True
        need = set()
        for o in self.ops:
            for pidx in o.waits:
                po = self.ops[pidx]
                if po.dma_q is None:
                    need.add(pidx)
        mcount = {e: 0 for e in ENGINES}
        for e in ENGINES:
            c = 0
            for i in self.eng_ops[e]:
                if i in need:
                    c += 1
                    self.ops[i].mnum = c
            mcount[e] = c
        self.mcount = mcount

        msem = {}
        qsem = {}
        for e in ENGINES:
            msem[e] = self.ctx.enter_context(self.nc.semaphore(f"ms_{e}"))
        for q in DMA_QUEUES:
            qsem[q] = [self.ctx.enter_context(self.nc.semaphore(f"qs_{q}{i}"))
                       for i in range(DMA_SLOTS[q])]

        blk = self.ctx.enter_context(self.nc.Block())
        sections = {"pe": blk.tensor, "act": blk.scalar, "dve": blk.vector,
                    "pool": blk.gpsimd, "sp": blk.sync}

        nwaits = [0]

        def slot_of(q, ordn):
            return ordn % DMA_SLOTS[q], ordn // DMA_SLOTS[q] + 1

        def run_engine(e, eng_name):
            known = {x: 0 for x in ENGINES}
            knownq = {(q, s): 0 for q in DMA_QUEUES for s in range(DMA_SLOTS[q])}
            for i in self.eng_ops[eng_name]:
                o = self.ops[i]
                for pidx in o.waits:
                    po = self.ops[pidx]
                    if po.dma_q is not None:
                        s, use = slot_of(po.dma_q, po.dma_ord)
                        val = 16 * use
                        if knownq[(po.dma_q, s)] < val:
                            e.wait_ge(qsem[po.dma_q][s], val)
                            knownq[(po.dma_q, s)] = val
                            nwaits[0] += 1
                    else:
                        val = po.mnum
                        assert val is not None
                        if known[po.eng] < val:
                            e.wait_ge(msem[po.eng], val)
                            known[po.eng] = val
                            nwaits[0] += 1
                if o.dma_q is not None:
                    s, use = slot_of(o.dma_q, o.dma_ord)
                    if use > 1 and knownq[(o.dma_q, s)] < 16 * (use - 1):
                        e.wait_ge(qsem[o.dma_q][s], 16 * (use - 1))
                        knownq[(o.dma_q, s)] = 16 * (use - 1)
                        nwaits[0] += 1
                    inst = o.emit(e)
                    inst.then_inc(qsem[o.dma_q][s], 16)
                else:
                    inst = o.emit(e)
                    if o.mnum is not None:
                        inst.then_inc(msem[o.eng], 1)
            if eng_name == "sp":
                for q in DMA_QUEUES:
                    for s in range(DMA_SLOTS[q]):
                        tot = (self.dma_n[q] - s + DMA_SLOTS[q] - 1) // DMA_SLOTS[q]
                        if tot and knownq[(q, s)] < 16 * tot:
                            e.wait_ge(qsem[q][s], 16 * tot)

        for eng_name, dec in sections.items():
            def body(e, eng_name=eng_name):
                run_engine(e, eng_name)
            dec(body)

        self.nwaits = nwaits[0]
        self.ctx.close()
        return self.nc

    def stats(self):
        from collections import Counter
        c = Counter(o.eng for o in self.ops)
        return dict(c), {"waits": getattr(self, "nwaits", None), "ms": getattr(self, "mcount", None)}


D = 1024
B = 2
T = 8192
NBLK = 16
BLK = T // NBLK
NCORES = 8
TOK = 2 * B * BLK
TT = 512
NTT = TOK // TT
NCH = D // 128
EPS = 1e-5


def core_blocks(c):
    return (c, NBLK - 1 - c)


def inst_map(c):
    b0, b1 = core_blocks(c)
    return [(0, b0), (0, b1), (1, b0), (1, b1)]


def host_shard_x(x):
    out = []
    for c in range(NCORES):
        cols = []
        for (b, blk) in inst_map(c):
            cols.append(x[b, blk * BLK:(blk + 1) * BLK, :])
        xc = np.concatenate(cols, axis=0)
        out.append(np.ascontiguousarray(xc.T))
    return out


def host_unshard_x(slices):
    x = np.zeros((B, T, D), dtype=slices[0].dtype)
    for c in range(NCORES):
        xc = slices[c].T
        for i, (b, blk) in enumerate(inst_map(c)):
            x[b, blk * BLK:(blk + 1) * BLK, :] = xc[i * BLK:(i + 1) * BLK]
    return x


def host_tile_wT(w, kdim=None):
    O, K = w.shape
    assert O % 128 == 0 and K % 128 == 0
    noch, nkch = O // 128, K // 128
    t = w.reshape(noch, 128, nkch, 128).transpose(0, 3, 2, 1)
    return np.ascontiguousarray(t.reshape(noch * 128, nkch * 128))


def host_pack_cols(vectors):
    vectors = dict(vectors)
    vectors.setdefault("eps", np.full(128, EPS, np.float32))
    vectors.setdefault("negbigm", np.full(128, -20.0, np.float32))
    cols = []
    index = {}
    for name, v in vectors.items():
        v = np.asarray(v, dtype=np.float32).ravel()
        n = (len(v) + 127) // 128
        pad = np.zeros(n * 128, np.float32)
        pad[:len(v)] = v
        index[name] = (len(cols), n)
        for i in range(n):
            cols.append(pad[i * 128:(i + 1) * 128])
    blob = np.stack(cols, axis=1) if cols else np.zeros((128, 1), np.float32)
    return np.ascontiguousarray(blob), index


def host_consts():
    c = {}
    eye = np.eye(128, dtype=np.float32)
    c["eye"] = eye
    c["eye2"] = np.vstack([np.eye(64), np.eye(64)]).astype(np.float32)
    c["ones_col"] = np.ones((128, 1), np.float32)
    c["ones_blk"] = np.ones((128, 16), np.float32)
    s = np.arange(128)[:, None]
    t = np.arange(128)[None, :]
    c["lt_inc"] = (s <= t).astype(np.float32)
    c["lt_strict"] = (s < t).astype(np.float32)
    c["ge_inc"] = (s >= t).astype(np.float32)
    c["neg_inc"] = np.where(s <= t, 0, NEG).astype(np.float32)
    c["neg_inc_rev"] = np.where(s >= t, 0, NEG).astype(np.float32)
    c["causal_neg"] = np.where(t <= s, 0, NEG).astype(np.float32)
    blob = np.concatenate([v for v in c.values()], axis=1)
    idx = {}
    off = 0
    for k, v in c.items():
        idx[k] = (off, v.shape[1])
        off += v.shape[1]
    return np.ascontiguousarray(blob), idx


class Ctx:

    def __init__(self, p, core_id, cols_index, consts_index, cols_ncols, consts_ncols):
        self.p = p
        self.core = core_id
        self.X = [p.sbuf(f"X{c}", [128, TOK], F32) for c in range(NCH)]
        self.Z = [p.sbuf(f"Z{c}", [128, TOK], BF16) for c in range(NCH)]
        self.cols_t = p.dram_in("cols", [128, cols_ncols], F32)
        self.consts_t = p.dram_in("consts", [128, consts_ncols], F32)
        self.cols = p.sbuf("colsb", [128, cols_ncols], F32)
        self.consts = p.sbuf("constsb", [128, consts_ncols], F32)
        self.consts_bf = p.sbuf("constsbf", [128, consts_ncols], BF16)
        self.cols_index = cols_index
        self.consts_index = consts_index
        p.dma("sp", self.cols, self.cols_t)
        p.dma("sp", self.consts, self.consts_t)
        p.dma("pool", self.consts_bf, self.consts)
        self.wslab = [p.sbuf(f"wslab{i}", [128, 2048], BF16) for i in range(2)]

    def col(self, name, i=0):
        c0, n = self.cols_index[name]
        assert i < n, (name, i, n)
        return (self.cols, (slice(None), slice(c0 + i, c0 + i + 1)))

    def col1(self, name, i=0, rows=1):
        c0, n = self.cols_index[name]
        assert i < n, (name, i, n)
        return (self.cols, (slice(0, rows), slice(c0 + i, c0 + i + 1)))

    def cst(self, name, bf=False, rows=slice(None), cols=None):
        c0, n = self.consts_index[name]
        t = self.consts_bf if bf else self.consts
        cs = slice(c0, c0 + n) if cols is None else slice(c0 + cols.start, c0 + cols.stop)
        return (t, (rows, cs))

    def ones_row(self, n=128, bf=False):
        return self.cst("lt_inc", bf=bf, rows=slice(0, 1), cols=slice(0, n))


def emit_ln(p, C, out=None, src=None, gain=None, bias=None, tag="ln"):
    src = src or C.X
    out = out or C.Z
    ones = C.ones_row()
    ones_c = C.cst("ones_col", bf=(src[0].dtype == BF16))
    inv_d = 1.0 / D
    sc = p.scope()
    sc.__enter__()
    class S: pass
    C_ = S()
    C_.sq = p.sbuf("lnsq", [128, TT], F32)
    C_.tmp = p.sbuf("lntmp", [128, TT], F32)
    C_.r_mu = p.sbuf("r_mu", [1, TT], F32)
    C_.r_ex2 = p.sbuf("r_ex2", [1, TT], F32)
    C_.r_var = p.sbuf("r_var", [1, TT], F32)
    C_.r_std = p.sbuf("r_std", [1, TT], F32)
    C_.r_rstd = p.sbuf("r_rstd", [1, TT], F32)
    C_.r_mur = p.sbuf("r_mur", [1, TT], F32)
    C_.ps_sum = p.psum("ps_sum", [1, TT])
    C_.ps_sq = p.psum("ps_sq", [1, TT])
    C_.ps_b0 = p.psum("ps_b0", [128, TT])
    C_.ps_b1 = p.psum("ps_b1", [128, TT])
    for tt in range(NTT):
        ts = slice(tt * TT, (tt + 1) * TT)
        for c in range(NCH):
            p.mm(C_.ps_sum, ones_c, (src[c], (slice(None), ts)),
                 start=(c == 0), stop=(c == NCH - 1), tag=tag + "_sum")
        for c in range(NCH):
            p.act(C_.sq, (src[c], (slice(None), ts)), AF.Square, tag=tag + "_sq")
            p.mm(C_.ps_sq, C.cst("ones_col"), C_.sq, start=(c == 0), stop=(c == NCH - 1), tag=tag + "_sqs")
        p.vts(C_.r_mu, C_.ps_sum, inv_d, ALU.mult, tag=tag + "_mu")
        p.vts(C_.r_ex2, C_.ps_sq, inv_d, ALU.mult, tag=tag + "_ex2")
        p.vtt(C_.r_var, C_.r_mu, C_.r_mu, ALU.mult, tag=tag + "_musq")
        p.vtt(C_.r_var, C_.r_ex2, C_.r_var, ALU.subtract, tag=tag + "_var")
        p.act(C_.r_std, C_.r_var, AF.Sqrt, bias=C.col1("eps"), tag=tag + "_std")
        p.vrecip(C_.r_rstd, C_.r_std, tag=tag + "_rstd")
        p.vtt(C_.r_mur, C_.r_mu, C_.r_rstd, ALU.mult, tag=tag + "_mur")
        p.mm(C_.ps_b0, ones, C_.r_rstd, tag=tag + "_b0")
        p.mm(C_.ps_b1, ones, C_.r_mur, tag=tag + "_b1")
        for c in range(NCH):
            p.vtt(C_.tmp, (src[c], (slice(None), ts)), C_.ps_b0, ALU.mult, tag=tag + "_ap0")
            dst = (out[c], (slice(None), ts))
            if gain is None:
                p.vtt(dst, C_.tmp, C_.ps_b1, ALU.subtract, tag=tag + "_ap1")
            else:
                p.vtt(C_.tmp, C_.tmp, C_.ps_b1, ALU.subtract, tag=tag + "_ap1")
                p.vts(dst, C_.tmp, C.col(gain, c), ALU.mult, C.col(bias, c), ALU.add, tag=tag + "_aff")
    sc.__exit__(None, None, None)


def proj_stream(p, C, w_dram, noch, nkch, x_tiles, out_cb, bias=None, xslice=None, tag="proj"):
    with p.scope():
        ps = [[p.psum(f"{tag}_ps", [128, TT]) for _ in range(NTT)] for _ in range(2)]
        for m in range(noch):
            slab = C.wslab[m % 2]
            p.dma("pool", (slab, (slice(None), slice(0, nkch * 128))),
                  (w_dram, (slice(m * 128, (m + 1) * 128), slice(None))), tag=tag + "_wdma")
            for tt in range(NTT):
                ts = slice(tt * TT, (tt + 1) * TT)
                for kc in range(nkch):
                    p.mm(ps[m % 2][tt], (slab, (slice(None), slice(kc * 128, (kc + 1) * 128))),
                         (x_tiles[kc], (slice(None), ts)),
                         start=(kc == 0), stop=(kc == nkch - 1), tag=tag + "_mm")
                out_cb(m, tt, ps[m % 2][tt])


def proj_res(p, C, w_tiles, noch, nkch, x_tiles, out_cb, tag="projr"):
    with p.scope():
        ps = [[p.psum(f"{tag}_ps", [128, TT]) for _ in range(NTT)] for _ in range(2)]
        for m in range(noch):
            for tt in range(NTT):
                ts = slice(tt * TT, (tt + 1) * TT)
                for kc in range(nkch):
                    p.mm(ps[m % 2][tt], (w_tiles[kc], (slice(None), slice(m * 128, (m + 1) * 128))),
                         (x_tiles[kc], (slice(None), ts)),
                         start=(kc == 0), stop=(kc == nkch - 1), tag=tag + "_mm")
                out_cb(m, tt, ps[m % 2][tt])


def res_add_cb(p, C, bias_name=None, scale=None):
    def cb(m, tt, ps):
        ts = slice(tt * TT, (tt + 1) * TT)
        dst = (C.X[m], (slice(None), ts))
        if bias_name is not None:
            p.vstt(dst, ps, C.col(bias_name, m), dst, ALU.add, ALU.add, tag="res_add")
        else:
            p.vtt(dst, ps, dst, ALU.add, tag="res_add")
    return cb


def build_l1ffn_weight(p, C):
    ld = p.dram_in("l1f_ld", [4, 100], F32)
    lu = p.dram_in("l1f_lu", [50, 4], F32)
    upT = p.dram_in("l1f_upT", [50, 1024], F32)
    da = p.dram_in("l1f_da", [100, 1024], F32)
    wT = p.dram_in("l1f_wT", [NCH * 128, 1024], F32)
    MT = [p.sbuf("MT", [128, 1024], BF16) for _ in range(NCH)]
    with p.scope():
        s_ld = p.sbuf("s_ld", [4, 100], F32)
        s_lu = p.sbuf("s_lu", [50, 4], F32)
        s_upT = p.sbuf("s_upT", [50, 1024], F32)
        s_da = p.sbuf("s_da", [100, 1024], F32)
        s_w = p.sbuf("s_w", [128, 1024], F32)
        b1 = p.sbuf("b1", [4, 1024], F32)
        b2 = p.sbuf("b2", [100, 1024], F32)
        ps = p.psum("lfw_ps", [128, 512])
        p.dma("sp", s_ld, ld)
        p.dma("sp", s_lu, lu)
        p.dma("sp", s_upT, upT)
        p.dma("sp", s_da, da)
        for h in range(2):
            ns = slice(h * 512, (h + 1) * 512)
            p.mm((ps, (slice(0, 4), slice(None))), s_lu, (s_upT, (slice(None), ns)), tag="lfw1")
            p.scopy((b1, (slice(0, 4), ns)), (ps, (slice(0, 4), slice(None))), tag="lfw1e")
        for h in range(2):
            ns = slice(h * 512, (h + 1) * 512)
            p.mm((ps, (slice(0, 100), slice(None))), s_ld, (b1, (slice(None), ns)), tag="lfw2")
            p.scopy((b2, (slice(0, 100), ns)), (ps, (slice(0, 100), slice(None))), tag="lfw2e")
        for m in range(NCH):
            p.dma("sp", s_w, (wT, (slice(m * 128, (m + 1) * 128), slice(None))), tag="lfw_w")
            for h in range(2):
                ns = slice(h * 512, (h + 1) * 512)
                p.mm(ps, (s_da, (slice(None), slice(m * 128, (m + 1) * 128))),
                     (b2, (slice(None), ns)), tag="lfw3")
                p.vtt((MT[m], (slice(None), ns)), ps, (s_w, (slice(None), ns)), ALU.add, tag="lfw3e")
    return MT


def emit_l1ffn(p, C, MT):
    emit_ln(p, C, tag="ln_l1f")
    proj_res(p, C, MT, NCH, NCH, C.Z, res_add_cb(p, C, bias_name="l1f_b"), tag="l1ffn")


def emit_load_stream(p, C, xin):
    for c in range(NCH):
        p.dma("sp", C.X[c], (xin, (slice(c * 128, (c + 1) * 128), slice(None))), tag="xload")


def emit_store_stream(p, C, xout, src=None):
    src = src or C.X
    for c in range(NCH):
        p.dma("sp", (xout, (slice(c * 128, (c + 1) * 128), slice(None))), src[c], tag="xstore")


def emit_l0ffn_weights_host(w12, w3, gate_w):
    E = w12.shape[0]
    slabs12 = []
    for e in range(E):
        t = host_tile_wT(w12[e])
        for hc in range(16):
            a = t[hc * 128:(hc + 1) * 128]
            bb = t[(16 + hc) * 128:(17 + hc) * 128]
            slabs12.append(np.concatenate([a, bb], axis=1))
    w12t = np.ascontiguousarray(np.concatenate(slabs12, axis=0))
    w3t = np.ascontiguousarray(np.concatenate([host_tile_wT(w3[e]) for e in range(E)], axis=0))
    gwt = np.ascontiguousarray(gate_w.T)
    return {"l0f_w12t": w12t, "l0f_w3t": w3t, "l0f_gwt": gwt}


def emit_l0ffn(p, C, tag="l0f"):
    NE = 8
    NHC = 16
    emit_ln(p, C, tag=tag + "_ln")
    w12t = p.dram_in("l0f_w12t", [NE * NHC * 128, 2048], F32) if not hasattr(C, "w12t") else C.w12t
    w3t = p.dram_in("l0f_w3t", [NE * 8 * 128, 2048], F32) if not hasattr(C, "w3t") else C.w3t
    gwt = p.dram_in("l0f_gwt", [1024, 8], F32) if not hasattr(C, "gwt") else C.gwt
    C.w12t, C.w3t, C.gwt = w12t, w3t, gwt
    eye = C.cst("eye")
    ones_row = C.ones_row()
    with p.scope():
        gwt_bf = p.sbuf("gwt_bf", [128, 8 * NCH], BF16)
        for kc in range(NCH):
            p.dma("pool", (gwt_bf, (slice(None), slice(kc * 8, kc * 8 + 8))),
                  (gwt, (slice(kc * 128, (kc + 1) * 128), slice(None))), tag=tag + "_gw")
        SC = p.sbuf("moe_sc", [128, 8], F32)
        M1 = p.sbuf("moe_m1", [128, 1], F32)
        NM1 = p.sbuf("moe_nm1", [128, 1], F32)
        GE = p.sbuf("moe_ge", [128, 8], F32)
        TMP8 = p.sbuf("moe_tmp8", [128, 8], F32)
        M2 = p.sbuf("moe_m2", [128, 1], F32)
        EE = p.sbuf("moe_e", [128, 8], F32)
        DEN = p.sbuf("moe_den", [128, 1], F32)
        RD = p.sbuf("moe_rd", [128, 1], F32)
        PP = p.sbuf("moe_p", [128, 8], F32)
        PT8 = p.sbuf("moe_pt8", [8, TOK], F32)
        PTe = p.sbuf("moe_pte", [1, TOK], F32)
        rsc = p.scope(); rsc.__enter__()
        ps_sc = p.psum("moe_ps_sc", [128, 8])
        ps_pt = p.psum("moe_ps_pt", [8, 128])
        for i in range(TOK // 128):
            ts = slice(i * 128, (i + 1) * 128)
            for kc in range(NCH):
                p.mm(ps_sc, (C.Z[kc], (slice(None), ts)),
                     (gwt_bf, (slice(None), slice(kc * 8, kc * 8 + 8))),
                     start=(kc == 0), stop=(kc == NCH - 1), tag=tag + "_scmm")
            p.scopy(SC, ps_sc, tag=tag + "_sc")
            p.vred(M1, SC, ALU.max, tag=tag + "_m1")
            p.vts(NM1, M1, -1.0, ALU.mult, tag=tag + "_nm1")
            p.vts(GE, SC, M1, ALU.is_ge, tag=tag + "_ge")
            p.vstt(TMP8, GE, NEG, SC, ALU.mult, ALU.add, tag=tag + "_tmp8")
            p.vred(M2, TMP8, ALU.max, tag=tag + "_m2")
            p.act(EE, SC, AF.Exp, bias=NM1, tag=tag + "_exp")
            p.vts(GE, SC, M2, ALU.is_ge, tag=tag + "_ge2")
            p.vstt(EE, EE, 1.0, GE, ALU.mult, ALU.mult, accum=DEN, tag=tag + "_emask")
            p.vrecip(RD, DEN, tag=tag + "_rd")
            p.vts(PP, EE, RD, ALU.mult, tag=tag + "_p")
            p.mm(ps_pt, PP, eye, tag=tag + "_ptmm")
            p.scopy((PT8, (slice(None), ts)), ps_pt, tag=tag + "_pt8")
        rsc.__exit__(None, None, None)
        hid = [p.sbuf("moe_hid", [128, TOK // 2], BF16) for _ in range(NHC)]
        pb_sb = p.sbuf("moe_pb", [128, TOK // 2], F32)
        sil = p.sbuf("moe_sil", [128, TT], F32)
        hf = p.sbuf("moe_hf", [128, TT], F32)
        PTe = p.sbuf("moe_pte", [1, TOK], F32)
        w3slab = [p.sbuf("moe_w3s", [128, 2048], BF16) for _ in range(2)]
        ps_h1 = [p.psum("moe_ps_h1", [128, TT]) for _ in range(2)]
        ps_h2 = [p.psum("moe_ps_h2", [128, TT]) for _ in range(2)]
        ps_pb = p.psum("moe_ps_pb", [128, TT])
        ps_o = [p.psum("moe_ps_o", [128, TT]) for _ in range(2)]
        for half in range(2):
            for e in range(NE):
                p.dma("pool", PTe, (PT8, (slice(e, e + 1), slice(None))), tag=tag + "_pte")
                for lt in range(2):
                    tt = half * 2 + lt
                    ts = slice(tt * TT, (tt + 1) * TT)
                    ls = slice(lt * TT, (lt + 1) * TT)
                    p.mm(ps_pb, ones_row, (PTe, (slice(0, 1), ts)), tag=tag + "_pbmm")
                    p.scopy((pb_sb, (slice(None), ls)), ps_pb, tag=tag + "_pbe")
                for hc in range(NHC):
                    slab = C.wslab[hc % 2]
                    r0 = (e * NHC + hc) * 128
                    p.dma("pool", slab, (w12t, (slice(r0, r0 + 128), slice(None))), tag=tag + "_w12d")
                    for lt in range(2):
                        tt = half * 2 + lt
                        ts = slice(tt * TT, (tt + 1) * TT)
                        ls = slice(lt * TT, (lt + 1) * TT)
                        for kc in range(NCH):
                            p.mm(ps_h1[lt], (slab, (slice(None), slice(kc * 128, (kc + 1) * 128))),
                                 (C.Z[kc], (slice(None), ts)), start=(kc == 0), stop=(kc == NCH - 1), tag=tag + "_h1")
                        for kc in range(NCH):
                            p.mm(ps_h2[lt], (slab, (slice(None), slice(1024 + kc * 128, 1024 + (kc + 1) * 128))),
                                 (C.Z[kc], (slice(None), ts)), start=(kc == 0), stop=(kc == NCH - 1), tag=tag + "_h2")
                        p.act(sil, ps_h1[lt], AF.Silu, tag=tag + "_sil")
                        p.vtt(hf, sil, ps_h2[lt], ALU.mult, tag=tag + "_h1h2")
                        p.vtt((hid[hc], (slice(None), ls)), hf, (pb_sb, (slice(None), ls)), ALU.mult, tag=tag + "_hp")
                for m in range(NCH):
                    slab3 = w3slab[m % 2]
                    r0 = (e * NCH + m) * 128
                    p.dma("pool", slab3, (w3t, (slice(r0, r0 + 128), slice(None))), tag=tag + "_w3d")
                    for lt in range(2):
                        tt = half * 2 + lt
                        ts = slice(tt * TT, (tt + 1) * TT)
                        ls = slice(lt * TT, (lt + 1) * TT)
                        for kc in range(NHC):
                            p.mm(ps_o[lt], (slab3, (slice(None), slice(kc * 128, (kc + 1) * 128))),
                                 (hid[kc], (slice(None), ls)), start=(kc == 0), stop=(kc == NHC - 1), tag=tag + "_w3")
                        p.vtt((C.X[m], (slice(None), ts)), ps_o[lt],
                              (C.X[m], (slice(None), ts)), ALU.add, tag=tag + "_acc")


def emit_l0attnb(p, C, tag="l0b"):
    SCALE = 0.125
    emit_ln(p, C, tag=tag + "_ln")
    wqkv = p.dram_in("l0b_wqkv", [1024, 3072], F32)
    wo = p.dram_in("l0b_wo", [1024, 1024], F32)
    with p.scope():
        WQKV = [p.sbuf("l0b_w", [128, 3072], BF16) for _ in range(NCH)]
        for kc in range(NCH):
            p.dma("pool", WQKV[kc], (wqkv, (slice(kc * 128, (kc + 1) * 128), slice(None))), tag=tag + "_wd")
        Exb = p.sbuf("l0b_ex", [16, 1024], BF16)
        with p.scope():
            exi = p.sbuf("l0b_exi", [16, 1024], I32)
            exp_ = p.sbuf("l0b_exp", [16, 1], I32)
            exf = p.sbuf("l0b_exf", [16, 1024], F32)
            exc = p.sbuf("l0b_exc", [16, 1], F32)
            p.iota(exi, pattern=[[1, 16], [0, 64]], base=0, channel_multiplier=0, tag=tag + "_i1")
            p.iota(exp_, pattern=[[0, 1]], base=0, channel_multiplier=1, tag=tag + "_i2")
            p.vcopy(exf, exi, tag=tag + "_i3")
            p.vcopy(exc, exp_, tag=tag + "_i4")
            p.vts(Exb, exf, exc, ALU.is_equal, tag=tag + "_i5")
        x2q = [p.sbuf("l0b_x2q", [128, 2048], BF16) for _ in range(2)]
        x2k = [p.sbuf("l0b_x2k", [128, 2048], BF16) for _ in range(2)]
        vtm = [p.sbuf("l0b_v", [128, 1024], BF16) for _ in range(2)]
        yb = p.sbuf("l0b_y", [128, 2048], BF16)
        zt = p.sbuf("l0b_zt", [128, 2048], BF16)
        s_ = {(i, j): p.sbuf("l0b_s", [128, 16], F32) for i in range(2) for j in range(2)}
        s_tmp = p.sbuf("l0b_stmp", [128, 16], F32)
        den = p.sbuf("l0b_den", [128, 16], F32)
        rd = p.sbuf("l0b_rd", [128, 16], F32)
        aa = [p.sbuf("l0b_a", [128, 16], F32) for _ in range(2)]
        sfm = [p.sbuf("l0b_sfm", [16, 128], BF16) for _ in range(2)]
        m1 = p.sbuf("l0b_m1", [128, 512], F32)
        attn = p.sbuf("l0b_attn", [128, 1024], BF16)
        avfm = p.sbuf("l0b_avfm", [128, 8 * 128], BF16)
        woslab = [p.sbuf("l0b_wos", [128, 1024], BF16) for _ in range(2)]
        TS = p.sbuf("l0b_ts", [128, 8 * 128], F32)
        sq = p.sbuf("l0b_sq", [128, 128], F32)
        r_mu = p.sbuf("l0b_rmu", [1, 128], F32)
        r_ex2 = p.sbuf("l0b_rex2", [1, 128], F32)
        r_var = p.sbuf("l0b_rvar", [1, 128], F32)
        r_std = p.sbuf("l0b_rstd", [1, 128], F32)
        r_rstd = p.sbuf("l0b_rrstd", [1, 128], F32)
        r_mur = p.sbuf("l0b_rmur", [1, 128], F32)
        tmp = p.sbuf("l0b_tmp", [128, 128], F32)
        tmp2 = p.sbuf("l0b_tmp2", [128, 128], F32)
        eye = C.cst("eye")
        eye_bf = C.cst("eye", bf=True)
        ones_col = C.cst("ones_col")
        ones_row = C.ones_row()
        for u in range(8):
            with p.scope():
                ps6 = [p.psum("l0b_ps6", [128, 512]) for _ in range(6)]
                for ci, ch in ((0, u), (1, u + 8)):
                    cs = slice(ch * 128, (ch + 1) * 128)
                    for half in range(6):
                        for kc in range(NCH):
                            p.mm(ps6[half], (C.Z[kc], (slice(None), cs)),
                                 (WQKV[kc], (slice(None), slice(half * 512, (half + 1) * 512))),
                                 start=(kc == 0), stop=(kc == NCH - 1), tag=tag + "_qkv")
                    x2qv = x2q[ci].h.rearrange("p (h f) -> p h f", h=16)
                    x2kv = x2k[ci].h.rearrange("p (h f) -> p h f", h=16)
                    for half in range(2):
                        hs = slice(half * 8, half * 8 + 8)
                        src = ps6[half]
                        p.raw("act", lambda e, o=(x2qv[:, hs, 0:64], x2q[ci].key), s=src: e.activation(o[0], s[:], AF.Relu), [src.key], [x2q[ci].key], tag + "_r1")
                        p.raw("act", lambda e, o=(x2qv[:, hs, 64:128], x2q[ci].key), s=src: e.activation(o[0], s[:], AF.Relu, scale=-1.0), [src.key], [x2q[ci].key], tag + "_r2")
                    for half in range(2):
                        hs = slice(half * 8, half * 8 + 8)
                        src = ps6[2 + half]
                        p.raw("act", lambda e, o=x2kv[:, hs, 0:64], s=src: e.activation(o, s[:], AF.Relu), [src.key], [x2k[ci].key], tag + "_r3")
                        p.raw("act", lambda e, o=x2kv[:, hs, 64:128], s=src: e.activation(o, s[:], AF.Relu, scale=-1.0), [src.key], [x2k[ci].key], tag + "_r4")
                    for half in range(2):
                        p.scopy((vtm[ci], (slice(None), slice(half * 512, (half + 1) * 512))), ps6[4 + half], tag=tag + "_v")
            for i in range(2):
                for j in range(2):
                    yv = yb.h.rearrange("p (h f) -> p h f", h=16)
                    zv = zt.h.rearrange("p (h f) -> p h f", h=16)
                    p.vtt(yb, x2q[i], x2k[j], ALU.mult, tag=tag + "_y")
                    sij = s_[(i, j)]
                    for r in (1, 2, 3):
                        p.raw("dve", lambda e, a=yv[:, :, r:128], b=yv[:, :, 0:128 - r], o=zv[:, :, r:128]: e.tensor_tensor(o, a, b, ALU.mult), [yb.key], [zt.key], tag + "_sg1")
                        p.raw("dve", lambda e, a=yv[:, :, 0:r], b=yv[:, :, 128 - r:128], o=zv[:, :, 0:r]: e.tensor_tensor(o, a, b, ALU.mult), [yb.key], [zt.key], tag + "_sg2")
                        dst = sij if r == 1 else s_tmp
                        p.raw("dve", lambda e, o=dst, z=zv: e.tensor_reduce(o[:], z[:, :, :], AX.X, ALU.add), [zt.key], [dst.key], tag + "_red")
                        if r > 1:
                            p.vtt(sij, sij, s_tmp, ALU.add, tag=tag + "_sacc")
            for i in range(2):
                sc_ep = p.scope(); sc_ep.__enter__()
                ps_tr = p.psum("l0b_pstr", [16, 128])
                ps_b = [p.psum("l0b_psb", [128, 512]) for _ in range(4)]
                ps_o = [p.psum("l0b_pso", [128, 512]) for _ in range(2)]
                ps_trb = [p.psum("l0b_pstrb", [128, 128], BF16)]
                if True:
                    ch = u if i == 0 else u + 8
                    cs = slice(ch * 128, (ch + 1) * 128)
                    p.vtt(den, s_[(i, 0)], s_[(i, 1)], ALU.add, tag=tag + "_den")
                    p.vts(den, den, SCALE, ALU.mult, EPS, ALU.add, tag=tag + "_den2")
                    p.vrecip(rd, den, tag=tag + "_rd")
                    p.vstt(aa[0], s_[(i, 0)], SCALE, rd, ALU.mult, ALU.mult, tag=tag + "_a0")
                    p.vstt(aa[1], s_[(i, 1)], SCALE, rd, ALU.mult, ALU.mult, tag=tag + "_a1")
                    for jj in range(2):
                        p.transpose(ps_tr, aa[jj], eye, tag=tag + "_str")
                        p.scopy(sfm[jj], ps_tr, tag=tag + "_sfe")
                        for half in range(2):
                            p.mm(ps_b[jj * 2 + half], sfm[jj],
                                 (Exb, (slice(None), slice(half * 512, (half + 1) * 512))), tag=tag + "_bc")
                    for half in range(2):
                        hs = slice(half * 512, (half + 1) * 512)
                        p.vtt(m1, ps_b[half], (vtm[0], (slice(None), hs)), ALU.mult, tag=tag + "_m1")
                        p.vtt((attn, (slice(None), hs)), ps_b[2 + half], (vtm[1], (slice(None), hs)), ALU.mult, tag=tag + "_m2")
                        p.vtt((attn, (slice(None), hs)), m1, (attn, (slice(None), hs)), ALU.add, tag=tag + "_msum")
                    for c in range(NCH):
                        p.transpose(ps_trb[0], (attn, (slice(None), slice(c * 128, (c + 1) * 128))), eye_bf, tag=tag + "_avt")
                        p.scopy((avfm, (slice(None), slice(c * 128, (c + 1) * 128))), ps_trb[0], tag=tag + "_ave")
                    for kc in range(NCH):
                        p.dma("pool", woslab[kc % 2], (wo, (slice(kc * 128, (kc + 1) * 128), slice(None))), tag=tag + "_wod")
                        for oc in range(NCH):
                            g, off = divmod(oc, 4)
                            p.mm((ps_o[g], (slice(None), slice(off * 128, (off + 1) * 128))),
                                 (woslab[kc % 2], (slice(None), slice(oc * 128, (oc + 1) * 128))),
                                 (avfm, (slice(None), slice(kc * 128, (kc + 1) * 128))),
                                 start=(kc == 0 and off == 0), stop=(kc == NCH - 1 and off == 3), tag=tag + "_wo")
                    for c in range(NCH):
                        g, off = divmod(c, 4)
                        p.vtt((TS, (slice(None), slice(c * 128, (c + 1) * 128))),
                              (ps_o[g], (slice(None), slice(off * 128, (off + 1) * 128))),
                              (C.Z[c], (slice(None), cs)), ALU.add, tag=tag + "_tsum")
                    sc_ep.__exit__(None, None, None)
                    with p.scope():
                        ps_s = p.psum("l0b_ps_s", [1, 128])
                        ps_q = p.psum("l0b_ps_q", [1, 128])
                        ps_b0 = p.psum("l0b_ps_b0", [128, 128])
                        ps_b1 = p.psum("l0b_ps_b1", [128, 128])
                        for c in range(NCH):
                            p.mm(ps_s, ones_col, (TS, (slice(None), slice(c * 128, (c + 1) * 128))),
                                 start=(c == 0), stop=(c == NCH - 1), tag=tag + "_lns")
                        for c in range(NCH):
                            p.act(sq, (TS, (slice(None), slice(c * 128, (c + 1) * 128))), AF.Square, tag=tag + "_lnq")
                            p.mm(ps_q, ones_col, sq, start=(c == 0), stop=(c == NCH - 1), tag=tag + "_lnq2")
                        p.vts(r_mu, ps_s, 1.0 / D, ALU.mult, tag=tag + "_mu")
                        p.vts(r_ex2, ps_q, 1.0 / D, ALU.mult, tag=tag + "_ex2")
                        p.vtt(r_var, r_mu, r_mu, ALU.mult, tag=tag + "_musq")
                        p.vtt(r_var, r_ex2, r_var, ALU.subtract, tag=tag + "_var")
                        p.act(r_std, r_var, AF.Sqrt, bias=C.col1("eps"), tag=tag + "_sqrt")
                        p.vrecip(r_rstd, r_std, tag=tag + "_rr")
                        p.vtt(r_mur, r_mu, r_rstd, ALU.mult, tag=tag + "_mr")
                        p.mm(ps_b0, ones_row, r_rstd, tag=tag + "_bc0")
                        p.mm(ps_b1, ones_row, r_mur, tag=tag + "_bc1")
                        for c in range(NCH):
                            p.vtt(tmp, (TS, (slice(None), slice(c * 128, (c + 1) * 128))), ps_b0, ALU.mult, tag=tag + "_ap0")
                            p.vtt(tmp, tmp, ps_b1, ALU.subtract, tag=tag + "_ap1")
                            p.vts(tmp2, tmp, C.col("l0b_g", c), ALU.mult, C.col("l0b_b", c), ALU.add, tag=tag + "_aff")
                            p.vtt((C.X[c], (slice(None), cs)), tmp2, (C.X[c], (slice(None), cs)), ALU.add, tag=tag + "_res")


def emit_ln_win(p, C, src, out, c0, w, gain=None, bias=None, tag="lnw"):
    ones_col = C.cst("ones_col", bf=(src[0].dtype == BF16))
    ones_row = C.ones_row()
    cs = slice(c0, c0 + w)
    with p.scope():
        sq = p.sbuf("lw_sq", [128, w], F32)
        tmp = p.sbuf("lw_tmp", [128, w], F32)
        r_mu = p.sbuf("lw_mu", [1, w], F32)
        r_ex2 = p.sbuf("lw_ex2", [1, w], F32)
        r_var = p.sbuf("lw_var", [1, w], F32)
        r_std = p.sbuf("lw_std", [1, w], F32)
        r_rstd = p.sbuf("lw_rstd", [1, w], F32)
        r_mur = p.sbuf("lw_mur", [1, w], F32)
        ps_sum = p.psum("lw_pss", [1, w])
        ps_sq = p.psum("lw_psq", [1, w])
        ps_b0 = p.psum("lw_psb0", [128, w])
        ps_b1 = p.psum("lw_psb1", [128, w])
        for c in range(NCH):
            p.mm(ps_sum, ones_col, (src[c], (slice(None), cs)), start=(c == 0), stop=(c == NCH - 1), tag=tag + "_s")
        for c in range(NCH):
            p.act(sq, (src[c], (slice(None), cs)), AF.Square, tag=tag + "_q")
            p.mm(ps_sq, C.cst("ones_col"), sq, start=(c == 0), stop=(c == NCH - 1), tag=tag + "_q2")
        p.vts(r_mu, ps_sum, 1.0 / D, ALU.mult, tag=tag + "_mu")
        p.vts(r_ex2, ps_sq, 1.0 / D, ALU.mult, tag=tag + "_e2")
        p.vtt(r_var, r_mu, r_mu, ALU.mult, tag=tag + "_m2")
        p.vtt(r_var, r_ex2, r_var, ALU.subtract, tag=tag + "_v")
        p.act(r_std, r_var, AF.Sqrt, bias=C.col1("eps"), tag=tag + "_sd")
        p.vrecip(r_rstd, r_std, tag=tag + "_rs")
        p.vtt(r_mur, r_mu, r_rstd, ALU.mult, tag=tag + "_mr")
        p.mm(ps_b0, ones_row, r_rstd, tag=tag + "_b0")
        p.mm(ps_b1, ones_row, r_mur, tag=tag + "_b1")
        for c in range(NCH):
            p.vtt(tmp, (src[c], (slice(None), cs)), ps_b0, ALU.mult, tag=tag + "_a0")
            dst = (out[c], (slice(None), slice(0, w))) if out[c].shape[1] == w else (out[c], (slice(None), cs))
            if gain is None:
                p.vtt(dst, tmp, ps_b1, ALU.subtract, tag=tag + "_a1")
            else:
                p.vtt(tmp, tmp, ps_b1, ALU.subtract, tag=tag + "_a1")
                p.vts(dst, tmp, C.col(gain, c), ALU.mult, C.col(bias, c), ALU.add, tag=tag + "_af")


def emit_l0attna_passA(p, C, fq, fk, fv, states, tag="l0aA"):
    wqkvg = p.dram_in("l0a_qkvgt", [32 * 128, 1024], F32)
    emit_ln(p, C, tag=tag + "_ln")
    with p.scope():
        z2 = [p.sbuf("l0a_z2", [128, BLK], BF16) for _ in range(NCH)]
        qf = [p.sbuf("l0a_qf", [128, BLK], BF16) for _ in range(NCH)]
        kf = [p.sbuf("l0a_kf", [128, BLK], BF16) for _ in range(NCH)]
        vf = [p.sbuf("l0a_vf", [128, BLK], BF16) for _ in range(NCH)]
        gf = [p.sbuf("l0a_gf", [128, BLK], BF16) for _ in range(NCH)]
        wslab = [p.sbuf("l0a_ws", [128, 1024], BF16) for _ in range(2)]
        t1 = p.sbuf("l0a_t1", [128, BLK], F32)
        t2 = p.sbuf("l0a_t2", [128, BLK], F32)
        te = p.sbuf("l0a_te", [128, BLK], F32)
        gm = p.sbuf("l0a_gm", [1, BLK], F32)
        Sg = p.sbuf("l0a_sg", [128, 8 * 65], F32)
        ktm = p.sbuf("l0a_ktm", [128, 64], BF16)
        vtm = p.sbuf("l0a_vtm", [128, 64], BF16)
        for inst in range(4):
            c0 = inst * BLK
            cs = slice(c0, c0 + BLK)
            emit_ln_win(p, C, C.Z, z2, c0, BLK, gain="l0a_lng", bias="l0a_lnb", tag=tag + "_ln2")
            with p.scope():
                ps = [p.psum("l0a_ps", [128, BLK]) for _ in range(2)]
                ps_gm = p.psum("l0a_psgm", [1, BLK])
                ps_gnb = p.psum("l0a_psgnb", [128, BLK])
                for m in range(32):
                    slab = wslab[m % 2]
                    p.dma("pool", slab, (wqkvg, (slice(m * 128, (m + 1) * 128), slice(None))), tag=tag + "_wd")
                    xin = z2 if m < 24 else C.Z
                    for kc in range(NCH):
                        p.mm(ps[m % 2], (slab, (slice(None), slice(kc * 128, (kc + 1) * 128))),
                             (xin[kc], (slice(None), slice(0, BLK) if m < 24 else cs)),
                             start=(kc == 0), stop=(kc == NCH - 1), tag=tag + "_mm")
                    if m < 8:
                        p.vts(qf[m], ps[m % 2], C.col("l0a_qkvb", m), ALU.add, tag=tag + "_qb")
                    elif m < 16:
                        p.vts(kf[m - 8], ps[m % 2], C.col("l0a_qkvb", m), ALU.add, tag=tag + "_kb")
                    elif m < 24:
                        p.vts(vf[m - 16], ps[m % 2], C.col("l0a_qkvb", m), ALU.add, tag=tag + "_vb")
                    else:
                        p.act(gf[m - 24], ps[m % 2], AF.Sigmoid, bias=C.col("l0a_gateb", m - 24), tag=tag + "_sig")
                for c in range(NCH):
                    p.mm(ps_gm, C.cst("ones_col", bf=True), gf[c], start=(c == 0), stop=(c == NCH - 1), tag=tag + "_gs")
                p.vts(gm, ps_gm, 1.0 / D, ALU.mult, EPS, ALU.add, tag=tag + "_gm")
                p.vrecip(gm, gm, tag=tag + "_gr")
                p.mm(ps_gnb, C.ones_row(), gm, tag=tag + "_gb")
                for which, ft in (("q", qf), ("k", kf)):
                    for c in range(NCH):
                        p.vtt(t1, ft[c], gf[c], ALU.mult, tag=tag + "_g1")
                        p.vtt(t2, t1, ps_gnb, ALU.mult, tag=tag + "_g2")
                        p.vts(t1, t2, 0.0, ALU.min, tag=tag + "_mn")
                        p.act(te, t1, AF.Exp, tag=tag + "_ex")
                        p.vts(t1, t2, 0.0, ALU.max, tag=tag + "_mx")
                        p.vtt(ft[c], te, t1, ALU.add, tag=tag + "_f")
                for c in range(NCH):
                    p.dma("sp", (fq, (slice(c * 128, (c + 1) * 128), cs)), qf[c], tag=tag + "_fqo")
                    p.dma("sp", (fk, (slice(c * 128, (c + 1) * 128), cs)), kf[c], tag=tag + "_fko")
                    p.dma("act", (fv, (slice(c * 128, (c + 1) * 128), cs)), vf[c], tag=tag + "_fvo")
            p.vmemset(Sg, 0.0, tag=tag + "_s0")
            with p.scope():
                ps_kt = p.psum("l0a_pskt", [128, 64], BF16)
                ps_vt = p.psum("l0a_psvt", [128, 64], BF16)
                ps_up = p.psum("l0a_psup", [128, 65])
                for ch in range(4):
                    ts = slice(ch * 128, (ch + 1) * 128)
                    for h in range(16):
                        oc, off = divmod(h, 2)
                        hs = slice(off * 64, (off + 1) * 64)
                        hcol = slice((h // 2) * 65, (h // 2) * 65 + 65)
                        hcv = slice((h // 2) * 65, (h // 2) * 65 + 64)
                        eyeb = C.cst("eye", bf=True, rows=hs, cols=slice(0, 64)) if off == 0 else C.cst("eye2", bf=True, rows=hs, cols=slice(0, 64))
                        p.transpose(ps_kt, (kf[oc], (hs, ts)), eyeb, tag=tag + "_tk")
                        p.scopy(ktm, ps_kt, tag=tag + "_ek")
                        p.transpose(ps_vt, (vf[oc], (hs, ts)), eyeb, tag=tag + "_tv")
                        p.scopy(vtm, ps_vt, tag=tag + "_ev")
                        p.mm((ps_up, (hs, slice(0, 64))), ktm, vtm, start=True, stop=False, tag=tag + "_up")
                        p.mm((ps_up, (hs, slice(64, 65))), ktm, C.cst("ones_col", bf=True), start=False, stop=True, tag=tag + "_up2")
                        p.vtt((Sg, (hs, hcol)), (ps_up, (hs, slice(0, 65))),
                              (Sg, (hs, hcol)), ALU.add, tag=tag + "_sa")
            p.dma("sp", (states, (slice(inst * 128, (inst + 1) * 128), slice(None))), Sg, tag=tag + "_so")


def emit_l0attna_passB(p, C, fq, fk, fv, allstates, tag="l0aB"):
    wproj = p.dram_in("l0a_projt", [8 * 128, 1024], F32)
    with p.scope():
        qf = [p.sbuf("l0aB_qf", [128, BLK], BF16) for _ in range(NCH)]
        kf = [p.sbuf("l0aB_kf", [128, BLK], BF16) for _ in range(NCH)]
        vf = [p.sbuf("l0aB_vf", [128, BLK], BF16) for _ in range(NCH)]
        AO = [p.sbuf("l0aB_ao", [128, BLK], BF16) for _ in range(NCH)]
        Sg = p.sbuf("l0aB_sg", [128, 8 * 65], F32)
        Sbf = p.sbuf("l0aB_sbf", [128, 8 * 65], BF16)
        Stmp = p.sbuf("l0aB_st", [128, 8 * 65], F32)
        ktm = p.sbuf("l0aB_ktm", [128, 64], BF16)
        vtm = p.sbuf("l0aB_vtm", [128, 64], BF16)
        stm = p.sbuf("l0aB_stm", [128, 128], BF16)
        numsb = p.sbuf("l0aB_num", [128, 128], F32)
        rowt = p.sbuf("l0aB_rt", [1, 128], F32)
        wslab = [p.sbuf("l0aB_ws", [128, 1024], BF16) for _ in range(2)]
        for inst in range(4):
            c0 = inst * BLK
            cs = slice(c0, c0 + BLK)
            for c in range(NCH):
                p.dma("sp", qf[c], (fq, (slice(c * 128, (c + 1) * 128), cs)), tag=tag + "_fqi")
                p.dma("sp", kf[c], (fk, (slice(c * 128, (c + 1) * 128), cs)), tag=tag + "_fki")
                p.dma("act", vf[c], (fv, (slice(c * 128, (c + 1) * 128), cs)), tag=tag + "_fvi")
            p.vmemset(Sg, 0.0, tag=tag + "_s0")
            for bp in range(2 * NBLK):
                r0 = bp * 128
                p.dma("pool", Stmp, (allstates, (slice(r0, r0 + 128), slice(None))), tag=tag + "_si")
                p.vstt(Sg, Stmp, C.col("l0a_ind", inst * 2 * NBLK + bp), Sg, ALU.mult, ALU.add, tag=tag + "_spre")
            p.vcopy(Sbf, Sg, tag=tag + "_sbf")
            with p.scope():
                ps_kt = p.psum("l0aB_pskt", [128, 64], BF16)
                ps_vt = p.psum("l0aB_psvt", [128, 64], BF16)
                ps_st = p.psum("l0aB_psst", [128, 128])
                ps_num = p.psum("l0aB_psnum", [128, 128])
                ps_den = p.psum("l0aB_psden", [1, 128])
                ps_up = p.psum("l0aB_psup", [128, 65])
                ps_div = p.psum("l0aB_psdiv", [128, 128])
                for ch in range(4):
                    ts = slice(ch * 128, (ch + 1) * 128)
                    for h in range(16):
                        oc, off = divmod(h, 2)
                        hs = slice(off * 64, (off + 1) * 64)
                        hcol = slice((h // 2) * 65, (h // 2) * 65 + 65)
                        hcv = slice((h // 2) * 65, (h // 2) * 65 + 64)
                        hcd = slice((h // 2) * 65 + 64, (h // 2) * 65 + 65)
                        eyeb = C.cst("eye", bf=True, rows=hs, cols=slice(0, 64)) if off == 0 else C.cst("eye2", bf=True, rows=hs, cols=slice(0, 64))
                        p.transpose(ps_kt, (kf[oc], (hs, ts)), eyeb, tag=tag + "_tk")
                        p.scopy(ktm, ps_kt, tag=tag + "_ek")
                        p.transpose(ps_vt, (vf[oc], (hs, ts)), eyeb, tag=tag + "_tv")
                        p.scopy(vtm, ps_vt, tag=tag + "_ev")
                        p.mm(ps_st, (kf[oc], (hs, ts)), (qf[oc], (hs, ts)), tag=tag + "_sc")
                        p.vtt(stm, ps_st, C.cst("lt_inc"), ALU.mult, tag=tag + "_msk")
                        p.mm((ps_num, (hs, slice(None))), vtm, stm, start=True, stop=False, tag=tag + "_o1")
                        p.mm((ps_num, (hs, slice(None))), (Sbf, (hs, hcv)), (qf[oc], (hs, ts)), start=False, stop=True, tag=tag + "_o2")
                        p.mm(ps_den, C.cst("ones_col", bf=True), stm, start=True, stop=False, tag=tag + "_d1")
                        p.mm(ps_den, (Sbf, (hs, hcd)), (qf[oc], (hs, ts)), start=False, stop=True, tag=tag + "_d2")
                        p.mm((ps_up, (hs, slice(0, 64))), ktm, vtm, start=True, stop=False, tag=tag + "_up")
                        p.mm((ps_up, (hs, slice(64, 65))), ktm, C.cst("ones_col", bf=True), start=False, stop=True, tag=tag + "_up2")
                        p.vtt((Sg, (hs, hcol)), (ps_up, (hs, slice(0, 65))), (Sg, (hs, hcol)), ALU.add, tag=tag + "_sa")
                        p.vcopy((Sbf, (hs, hcol)), (Sg, (hs, hcol)), tag=tag + "_sb")
                        p.vts(rowt, ps_den, EPS, ALU.add, tag=tag + "_de")
                        p.vrecip(rowt, rowt, tag=tag + "_dr")
                        p.mm(ps_div, C.ones_row(), rowt, tag=tag + "_db")
                        p.scopy((numsb, (hs, slice(None))), (ps_num, (hs, slice(None))), tag=tag + "_ne")
                        p.vtt((AO[oc], (hs, ts)), (numsb, (hs, slice(None))), (ps_div, (hs, slice(None))), ALU.mult, tag=tag + "_ao")
            with p.scope():
                ps = [p.psum("l0aB_psp", [128, BLK]) for _ in range(2)]
                for m in range(NCH):
                    slab = wslab[m % 2]
                    p.dma("pool", slab, (wproj, (slice(m * 128, (m + 1) * 128), slice(None))), tag=tag + "_pwd")
                    for kc in range(NCH):
                        p.mm(ps[m % 2], (slab, (slice(None), slice(kc * 128, (kc + 1) * 128))),
                             AO[kc], start=(kc == 0), stop=(kc == NCH - 1), tag=tag + "_pmm")
                    p.vstt((C.X[m], (slice(None), cs)), ps[m % 2], C.col("l0a_projb", m),
                           (C.X[m], (slice(None), cs)), ALU.add, ALU.add, tag=tag + "_res")


def host_l0a_weights(qkv_w, qkv_b, gate_w, gate_b, proj_w, proj_b, ln_g, ln_b):
    stacked = np.concatenate([qkv_w, gate_w], axis=0)
    return ({"l0a_qkvgt": host_tile_wT(stacked), "l0a_projt": host_tile_wT(proj_w)},
            {"l0a_lng": ln_g, "l0a_lnb": ln_b, "l0a_qkvb": qkv_b,
             "l0a_gateb": gate_b, "l0a_projb": proj_b})


def emit_l1attna_passA(p, C, qd, kd, vd, rowsd, momd, tag="l1aA"):
    wqkv = p.dram_in("l1a_qkvt", [24 * 128, 1024], F32)
    wtq = p.dram_in("l1a_wtqT", [1024, 2], F32)
    emit_ln(p, C, tag=tag + "_ln")
    with p.scope():
        wslab = [p.sbuf("l1a_ws", [128, 1024], BF16) for _ in range(2)]
        vstage = p.sbuf("l1a_vst", [128, TT], BF16)
        vtrst = [p.sbuf("l1a_vtr", [128, 128], BF16) for _ in range(2)]
        wtqs = p.sbuf("l1a_wtqs", [128, 2 * NCH], BF16)
        ra = p.sbuf("l1a_ra", [1, TOK], F32)
        rb = p.sbuf("l1a_rb", [1, TOK], F32)
        rr = p.sbuf("l1a_rr", [1, TOK], F32)
        rt = p.sbuf("l1a_rt", [1, TOK], F32)
        mom = p.sbuf("l1a_mom", [1, 24], F32)
        t1 = p.sbuf("l1a_t1", [1, 1024], F32)
        t2 = p.sbuf("l1a_t2", [1, 1024], F32)
        t3 = p.sbuf("l1a_t3", [1, 1024], F32)
        for kc in range(NCH):
            p.dma("pool", (wtqs, (slice(None), slice(kc * 2, kc * 2 + 2))),
                  (wtq, (slice(kc * 128, (kc + 1) * 128), slice(None))), tag=tag + "_wtqd")
        with p.scope():
            ps = [p.psum("l1a_ps", [128, TT]) for _ in range(2)]
            ps_t = p.psum("l1a_pst", [128, 128], BF16)
            ps_r = p.psum("l1a_psr", [1, TT])
            for m in range(24):
                slab = wslab[m % 2]
                p.dma("pool", slab, (wqkv, (slice(m * 128, (m + 1) * 128), slice(None))), tag=tag + "_wd")
                for tt in range(NTT):
                    ts = slice(tt * TT, (tt + 1) * TT)
                    pcur = ps[(m * NTT + tt) % 2]
                    for kc in range(NCH):
                        p.mm(pcur, (slab, (slice(None), slice(kc * 128, (kc + 1) * 128))),
                             (C.Z[kc], (slice(None), ts)),
                             start=(kc == 0), stop=(kc == NCH - 1), tag=tag + "_mm")
                    if m < 8:
                        p.vts(vstage, pcur, C.col("l1a_bq", m), ALU.add, tag=tag + "_qb")
                        p.dma("sp", (qd, (slice(m * 128, (m + 1) * 128), ts)), vstage, tag=tag + "_qo")
                    elif m < 16:
                        p.vts(vstage, pcur, C.col("l1a_bk", m - 8), ALU.add, tag=tag + "_kb")
                        p.dma("sp", (kd, (slice((m - 8) * 128, (m - 7) * 128), ts)), vstage, tag=tag + "_ko")
                    else:
                        p.vts(vstage, pcur, C.col("l1a_bv", m - 16), ALU.add, tag=tag + "_vb")
                        for sb in range(TT // 128):
                            p.transpose(ps_t, (vstage, (slice(None), slice(sb * 128, (sb + 1) * 128))),
                                        C.cst("eye", bf=True), tag=tag + "_vt")
                            p.scopy(vtrst[sb % 2], ps_t, tag=tag + "_ve")
                            p.dma("act", (vd, (slice(tt * TT + sb * 128, tt * TT + (sb + 1) * 128),
                                               slice((m - 16) * 128, (m - 15) * 128))), vtrst[sb % 2], tag=tag + "_vo")
            for row, tile in ((0, ra), (1, rb)):
                for tt in range(NTT):
                    ts = slice(tt * TT, (tt + 1) * TT)
                    for kc in range(NCH):
                        p.mm(ps_r, (wtqs, (slice(None), slice(kc * 2 + row, kc * 2 + row + 1))),
                             (C.Z[kc], (slice(None), ts)),
                             start=(kc == 0), stop=(kc == NCH - 1), tag=tag + "_qpm")
                    p.vts((tile, (slice(0, 1), ts)), ps_r, C.col1(f"l1a_btq{row}"), ALU.add, tag=tag + "_qpb")
        p.vtt(rr, ra, ra, ALU.mult, tag=tag + "_r1")
        p.vtt(rt, rb, rb, ALU.mult, tag=tag + "_r2")
        p.vtt(rr, rr, rt, ALU.add, tag=tag + "_r3")
        p.dma("sp", (rowsd, (slice(0, 1), slice(None))), ra, tag=tag + "_ro0")
        p.dma("sp", (rowsd, (slice(1, 2), slice(None))), rb, tag=tag + "_ro1")
        p.dma("sp", (rowsd, (slice(2, 3), slice(None))), rr, tag=tag + "_ro2")
        p.vts(rt, ra, -2.0, ALU.mult, tag=tag + "_r4")
        p.dma("sp", (rowsd, (slice(3, 4), slice(None))), rt, tag=tag + "_ro3")
        p.vts(rt, rb, -2.0, ALU.mult, tag=tag + "_r5")
        p.dma("sp", (rowsd, (slice(4, 5), slice(None))), rt, tag=tag + "_ro4")
        for bt in range(2):
            bs = slice(bt * 1024, (bt + 1) * 1024)
            av = (ra, (slice(0, 1), bs))
            bv = (rb, (slice(0, 1), bs))
            def red(i, srct):
                p.vred((mom, (slice(0, 1), slice(bt * 12 + i, bt * 12 + i + 1))), srct, ALU.add, tag=tag + "_mr")
            red(0, av)
            red(1, bv)
            p.vtt(t1, av, av, ALU.mult, tag=tag + "_ma")
            p.vtt(t2, bv, bv, ALU.mult, tag=tag + "_mb")
            red(2, t1)
            red(3, t2)
            p.vtt(t3, av, bv, ALU.mult, tag=tag + "_mc")
            red(4, t3)
            p.vtt(t3, t1, av, ALU.mult, tag=tag + "_md")
            red(5, t3)
            p.vtt(t3, t2, bv, ALU.mult, tag=tag + "_me")
            red(6, t3)
            p.vtt(t3, t1, bv, ALU.mult, tag=tag + "_mf")
            red(7, t3)
            p.vtt(t3, t2, av, ALU.mult, tag=tag + "_mg")
            red(8, t3)
            p.vtt(t3, t1, t1, ALU.mult, tag=tag + "_mh")
            red(9, t3)
            p.vtt(t3, t2, t2, ALU.mult, tag=tag + "_mi")
            red(10, t3)
            p.vtt(t3, t1, t2, ALU.mult, tag=tag + "_mj")
            red(11, t3)
        p.dma("sp", momd, mom, tag=tag + "_mo")


def host_l1a_rows(rows_percore):
    glob = np.zeros((2, 5, T), np.float32)
    for c in range(NCORES):
        r = rows_percore[c]
        for i, (bt, blk) in enumerate(inst_map(c)):
            glob[bt, :, blk * BLK:(blk + 1) * BLK] = r[:, i * BLK:(i + 1) * BLK]
    pos = np.arange(T, dtype=np.float32)
    ones = np.ones(T, np.float32)
    krows = np.stack([np.stack([ones, glob[bt, 2], glob[bt, 3], glob[bt, 4],
                                -ones, pos]) for bt in range(2)])
    qrows = []
    for c in range(NCORES):
        rows = []
        for i, (bt, blk) in enumerate(inst_map(c)):
            s = slice(blk * BLK, (blk + 1) * BLK)
            rows += [glob[bt, 2, s], ones[s], glob[bt, 0, s], glob[bt, 1, s],
                     pos[s], ones[s], glob[bt, 0, s], glob[bt, 1, s]]
        qrows.append(np.ascontiguousarray(np.stack(rows)))
    return np.ascontiguousarray(krows), qrows


def emit_l1attna_passB(p, C, qd, mom_all, k_all, v_all, tag="l1aB"):
    wo = p.dram_in("l1a_wot", [1024, 1024], F32)
    wfq = p.dram_in("l1a_wfqT", [2, 1024], F32)
    qrowsd = p.dram_in("l1a_qrows", [32, BLK], F32)
    krowsd = p.dram_in("l1a_krows", [2, 6, T], F32)
    SC1 = 1.0 / 32.0
    with p.scope():
        thr = p.sbuf("l1b_thr", [1, 2], F32)
        thr_col = [p.sbuf("l1b_thrc", [128, 1], F32) for _ in range(2)]
        moms = p.sbuf("l1b_moms", [8, 24], F32)
        msum = p.sbuf("l1b_msum", [1, 24], F32)
        sc1 = p.sbuf("l1b_sc1", [1, 12], F32)
        p.dma("sp", moms, mom_all, tag=tag + "_mi")
        with p.scope():
            ps_m = p.psum("l1b_psm", [1, 24])
            p.mm(ps_m, C.cst("ones_col", rows=slice(0, 8)), moms, tag=tag + "_msum")
            p.scopy(msum, ps_m, tag=tag + "_mse")
        NTOT = float(T)
        for bt in range(2):
            def M(i):
                return (msum, (slice(0, 1), slice(bt * 12 + i, bt * 12 + i + 1)))
            def S(i):
                return (sc1, (slice(0, 1), slice(i, i + 1)))
            p.vtt(S(0), M(2), M(3), ALU.add, tag=tag + "_t0")
            p.vtt(S(1), M(0), M(0), ALU.mult, tag=tag + "_t1")
            p.vtt(S(2), M(1), M(1), ALU.mult, tag=tag + "_t2")
            p.vtt(S(1), S(1), S(2), ALU.add, tag=tag + "_t3")
            p.vstt(S(1), S(0), NTOT, S(1), ALU.mult, ALU.subtract, tag=tag + "_t4")
            p.vts(S(1), S(1), 2.0, ALU.mult, tag=tag + "_t5")
            p.vts(S(2), M(11), 2.0, ALU.mult, tag=tag + "_t6")
            p.vtt(S(2), S(2), M(9), ALU.add, tag=tag + "_t7")
            p.vtt(S(2), S(2), M(10), ALU.add, tag=tag + "_t8")
            p.vtt(S(3), M(5), M(8), ALU.add, tag=tag + "_t9")
            p.vtt(S(4), M(7), M(6), ALU.add, tag=tag + "_t10")
            p.vts(S(5), S(2), 2.0 * NTOT, ALU.mult, tag=tag + "_u0")
            p.vtt(S(6), S(0), S(0), ALU.mult, tag=tag + "_u1")
            p.vstt(S(5), S(6), 2.0, S(5), ALU.mult, ALU.add, tag=tag + "_u2")
            p.vtt(S(6), S(3), M(0), ALU.mult, tag=tag + "_u3")
            p.vtt(S(7), S(4), M(1), ALU.mult, tag=tag + "_u4")
            p.vtt(S(6), S(6), S(7), ALU.add, tag=tag + "_u5")
            p.vstt(S(5), S(6), -8.0, S(5), ALU.mult, ALU.add, tag=tag + "_u6")
            p.vtt(S(6), M(2), M(2), ALU.mult, tag=tag + "_u7")
            p.vtt(S(7), M(3), M(3), ALU.mult, tag=tag + "_u8")
            p.vtt(S(6), S(6), S(7), ALU.add, tag=tag + "_u9")
            p.vtt(S(7), M(4), M(4), ALU.mult, tag=tag + "_u10")
            p.vstt(S(6), S(7), 2.0, S(6), ALU.mult, ALU.add, tag=tag + "_u11")
            p.vstt(S(5), S(6), 4.0, S(5), ALU.mult, ALU.add, tag=tag + "_u12")
            p.vts(S(6), S(1), 1.0 / (NTOT * NTOT), ALU.mult, tag=tag + "_v0")
            p.vtt(S(7), S(1), S(6), ALU.mult, tag=tag + "_v1")
            p.vtt(S(7), S(5), S(7), ALU.subtract, tag=tag + "_v2")
            p.vts(S(7), S(7), 1.0 / (NTOT * NTOT - 1.0), ALU.mult, tag=tag + "_v3")
            p.act(S(8), S(7), AF.Sqrt, tag=tag + "_v4")
            p.vstt((thr, (slice(0, 1), slice(bt, bt + 1))), S(8), 1.25, S(6), ALU.mult, ALU.add, tag=tag + "_v5")
        with p.scope():
            ps_tc = p.psum("l1b_pstc", [128, 1])
            for bt in range(2):
                p.mm(ps_tc, C.ones_row(), (thr, (slice(0, 1), slice(bt, bt + 1))), tag=tag + "_tcm")
                p.scopy(thr_col[bt], ps_tc, tag=tag + "_tce")
        cb = p.sbuf("l1b_cb", [128, 8], F32)
        for m in range(NCH):
            p.vstt((cb, (slice(None), slice(m, m + 1))), C.col("l1a_bfq", m), 0.1,
                   C.col("l1a_bo", m), ALU.mult, ALU.add, tag=tag + "_cb")
        WoR = [p.sbuf("l1b_wo", [128, 1024], BF16) for _ in range(NCH)]
        for m in range(NCH):
            p.dma("pool", WoR[m], (wo, (slice(m * 128, (m + 1) * 128), slice(None))), tag=tag + "_wod")
        wfq0 = p.sbuf("l1b_wfq0", [1, 1024], BF16)
        wfq1 = p.sbuf("l1b_wfq1", [1, 1024], BF16)
        p.dma("pool", wfq0, (wfq, (slice(0, 1), slice(None))), tag=tag + "_wfqd")
        p.dma("pool", wfq1, (wfq, (slice(1, 2), slice(None))), tag=tag + "_wfqe")
        qtile = p.sbuf("l1b_q", [128, NCH * BLK], BF16)
        ktile = p.sbuf("l1b_k", [128, NCH * TT], BF16)
        vtile = p.sbuf("l1b_v", [128, 4 * 1024], BF16)
        kb4 = p.sbuf("l1b_kb4", [4, TT], F32)
        kcau = p.sbuf("l1b_kcau", [2, TT], F32)
        qa4 = p.sbuf("l1b_qa4", [4, BLK], F32)
        qcau = p.sbuf("l1b_qcau", [2, BLK], F32)
        qra = p.sbuf("l1b_qra", [1, BLK], F32)
        qrb = p.sbuf("l1b_qrb", [1, BLK], F32)
        ae = p.sbuf("l1b_ae", [1, 128], BF16)
        be = p.sbuf("l1b_be", [1, 128], BF16)
        arow = p.sbuf("l1b_ar", [1, 128], F32)
        a1 = p.sbuf("l1b_a1", [128, TT], F32)
        sm = p.sbuf("l1b_sm", [128, TT], F32)
        Pb = p.sbuf("l1b_p", [128, TT], BF16)
        ptb = p.sbuf("l1b_ptb", [128, 128], BF16)
        lpart = p.sbuf("l1b_lp", [128, 1], F32)
        lrec = p.sbuf("l1b_lr", [128, 1], F32)
        ctm = p.sbuf("l1b_ctm", [128, 1024], BF16)
        cfm = p.sbuf("l1b_cfm", [128, NCH * 128], BF16)
        u1 = p.sbuf("l1b_u1", [128, 128], F32)
        for inst in range(4):
            bat = 0 if inst < 2 else 1
            c0 = inst * BLK
            for c in range(NCH):
                p.dma("sp", (qtile, (slice(None), slice(c * BLK, (c + 1) * BLK))),
                      (qd, (slice(c * 128, (c + 1) * 128), slice(c0, c0 + BLK))), tag=tag + "_qi")
            p.dma("sp", qa4, (qrowsd, (slice(inst * 8, inst * 8 + 4), slice(None))), tag=tag + "_qa")
            p.dma("sp", qcau, (qrowsd, (slice(inst * 8 + 4, inst * 8 + 6), slice(None))), tag=tag + "_qc")
            p.dma("sp", qra, (qrowsd, (slice(inst * 8 + 6, inst * 8 + 7), slice(None))), tag=tag + "_qra")
            p.dma("sp", qrb, (qrowsd, (slice(inst * 8 + 7, inst * 8 + 8), slice(None))), tag=tag + "_qrb")
            for half in range(2):
                qsubs = (2 * half, 2 * half + 1)
                with p.scope():
                    ps_ctx = [p.psum("l1b_psc", [128, 512]) for _ in range(4)]
                    ps_pt = p.psum("l1b_pspt", [128, 128], BF16)
                    lsums = [p.sbuf("l1b_ls", [128, 1], F32) for _ in range(2)]
                    for qi in range(2):
                        p.vmemset(lsums[qi], 0.0, tag=tag + "_l0")
                    sc_k = p.scope(); sc_k.__enter__()
                    ps_sc = p.psum("l1b_pssc", [128, TT])
                    ps_d = p.psum("l1b_psd", [128, TT])
                    ps_cau = p.psum("l1b_pscau", [128, TT])
                    for kt in range(NBLK):
                        kg = kt * TT
                        for c in range(NCH):
                            p.dma("sp", (ktile, (slice(None), slice(c * TT, (c + 1) * TT))),
                                  (k_all, (slice(bat, bat + 1), slice(c * 128, (c + 1) * 128), slice(kg, kg + TT))), tag=tag + "_ki")
                        for sb in range(4):
                            p.dma("act", (vtile, (slice(None), slice(sb * 1024, (sb + 1) * 1024))),
                                  (v_all, (slice(bat, bat + 1), slice(kg + sb * 128, kg + (sb + 1) * 128), slice(None))), tag=tag + "_vi")
                        p.dma("pool", kb4, (krowsd, (slice(bat, bat + 1), slice(0, 4), slice(kg, kg + TT))), tag=tag + "_kb")
                        p.dma("pool", kcau, (krowsd, (slice(bat, bat + 1), slice(4, 6), slice(kg, kg + TT))), tag=tag + "_kc")
                        for qi, qs in enumerate(qsubs):
                            qss = slice(qs * 128, (qs + 1) * 128)
                            for kc in range(NCH):
                                p.mm(ps_sc,
                                     (qtile, (slice(None), slice(kc * BLK + qs * 128, kc * BLK + qs * 128 + 128))),
                                     (ktile, (slice(None), slice(kc * TT, (kc + 1) * TT))),
                                     start=(kc == 0), stop=(kc == NCH - 1), tag=tag + "_sc")
                            p.mm(ps_d, (qa4, (slice(None), qss)), kb4, tag=tag + "_dm")
                            p.mm(ps_cau, (qcau, (slice(None), qss)), kcau, tag=tag + "_cm")
                            p.vts(a1, ps_d, thr_col[bat], ALU.is_gt, -30000.0, ALU.mult, tag=tag + "_a1")
                            p.vtt(sm, ps_sc, a1, ALU.add, tag=tag + "_sm")
                            p.vts(a1, ps_cau, 0.0, ALU.is_gt, -30000.0, ALU.mult, tag=tag + "_a2")
                            p.vtt(sm, sm, a1, ALU.add, tag=tag + "_sm2")
                            p.act(Pb, sm, AF.Exp, bias=C.col("negbigm"), scale=SC1, accum=lpart, tag=tag + "_exp")
                            p.vtt(lsums[qi], lsums[qi], lpart, ALU.add, tag=tag + "_lacc")
                            for sb in range(4):
                                p.transpose(ps_pt, (Pb, (slice(None), slice(sb * 128, (sb + 1) * 128))),
                                            C.cst("eye", bf=True), tag=tag + "_ptr")
                                p.scopy(ptb, ps_pt, tag=tag + "_pte")
                                for dh in range(2):
                                    p.mm(ps_ctx[qi * 2 + dh], ptb,
                                         (vtile, (slice(None), slice(sb * 1024 + dh * 512, sb * 1024 + (dh + 1) * 512))),
                                         start=(kt == 0 and sb == 0),
                                         stop=(kt == NBLK - 1 and sb == 3), tag=tag + "_ctx")
                    sc_k.__exit__(None, None, None)
                    for qi, qs in enumerate(qsubs):
                        qss = slice(qs * 128, (qs + 1) * 128)
                        p.vrecip(lrec, lsums[qi], tag=tag + "_lr")
                        for dh in range(2):
                            p.vts((ctm, (slice(None), slice(dh * 512, (dh + 1) * 512))),
                                  ps_ctx[qi * 2 + dh], lrec, ALU.mult, tag=tag + "_ctm")
                        for c in range(NCH):
                            p.transpose(ps_pt, (ctm, (slice(None), slice(c * 128, (c + 1) * 128))),
                                        C.cst("eye", bf=True), tag=tag + "_ctr")
                            p.scopy((cfm, (slice(None), slice(c * 128, (c + 1) * 128))), ps_pt, tag=tag + "_cte")
                        p.act(arow, (qa4, (slice(0, 1), qss)), AF.Sqrt, tag=tag + "_qn")
                        p.vts(arow, arow, 1e-12, ALU.max, tag=tag + "_qm")
                        p.vrecip(arow, arow, tag=tag + "_qrc")
                        p.vts(arow, arow, 0.1, ALU.mult, tag=tag + "_qsc")
                        p.vstt(ae, (qra, (slice(0, 1), qss)), 1.0, arow, ALU.mult, ALU.mult, tag=tag + "_ae")
                        p.vstt(be, (qrb, (slice(0, 1), qss)), 1.0, arow, ALU.mult, ALU.mult, tag=tag + "_be")
                        with p.scope():
                            ps_o = [p.psum("l1b_pso", [128, 512]) for _ in range(2)]
                            for kc in range(NCH):
                                for oc in range(NCH):
                                    g, off = divmod(oc, 4)
                                    p.mm((ps_o[g], (slice(None), slice(off * 128, (off + 1) * 128))),
                                         (WoR[kc], (slice(None), slice(oc * 128, (oc + 1) * 128))),
                                         (cfm, (slice(None), slice(kc * 128, (kc + 1) * 128))),
                                         start=(kc == 0 and off == 0), stop=False, tag=tag + "_wo")
                            for oc in range(NCH):
                                g, off = divmod(oc, 4)
                                p.mm((ps_o[g], (slice(None), slice(off * 128, (off + 1) * 128))),
                                     (wfq0, (slice(None), slice(oc * 128, (oc + 1) * 128))),
                                     ae, start=False, stop=False, tag=tag + "_fq")
                                p.mm((ps_o[g], (slice(None), slice(off * 128, (off + 1) * 128))),
                                     (wfq1, (slice(None), slice(oc * 128, (oc + 1) * 128))),
                                     be, start=False, stop=(oc in (3, 7)), tag=tag + "_fq2")
                            cs2 = slice(c0 + qs * 128, c0 + (qs + 1) * 128)
                            for oc in range(NCH):
                                g, off = divmod(oc, 4)
                                p.vstt(u1, (ps_o[g], (slice(None), slice(off * 128, (off + 1) * 128))),
                                       (cb, (slice(None), slice(oc, oc + 1))),
                                       (C.Z[oc], (slice(None), cs2)), ALU.add, ALU.add, tag=tag + "_zadd")
                                p.vtt((C.X[oc], (slice(None), cs2)), u1, (C.X[oc], (slice(None), cs2)), ALU.add, tag=tag + "_res")


def _mb_ln_halo(p, C, halod, HZ, tag):
    HX = [p.sbuf("mbh_x", [128, 24], F32) for _ in range(NCH)]
    for c in range(NCH):
        p.dma("sp", HX[c], (halod, (slice(c * 128, (c + 1) * 128), slice(None))), tag=tag + "_hl")
    emit_ln_win(p, C, HX, HZ, 0, 24, tag=tag + "_hln")


def emit_mamba_passA(p, C, dirn, halod, yd, ccd, etd, eed, stated, decd, tag=None):
    tag = tag or f"mb{dirn}A"
    wipb = p.dram_in(f"mb{dirn}_ipbT", [17 * 128, 1024], F32)
    fwd = dirn == "F"
    with p.scope():
        HZ = [p.sbuf("mbh_z", [128, 24], BF16) for _ in range(NCH)]
        _mb_ln_halo(p, C, halod, HZ, tag)
        EXT = [p.sbuf("mb_ext", [128, BLK + 6], BF16) for _ in range(17)]
        XS = [p.sbuf("mb_xs", [128, BLK], BF16) for _ in range(16)]
        Cext = p.sbuf("mb_cext", [16, BLK + 6], BF16)
        Bc = p.sbuf("mb_bc", [16, BLK], BF16)
        Cc = p.sbuf("mb_cc", [16, BLK], BF16)
        DT = p.sbuf("mb_dt", [64, BLK], F32)
        DTA = p.sbuf("mb_dta", [64, BLK], F32)
        LL = p.sbuf("mb_ll", [64, BLK], F32)
        ONES_BK = p.sbuf("mb_ones", [64, BLK], F32)
        ACOL = p.sbuf("mb_acol", [64, 1], F32)
        LSUM = p.sbuf("mb_lsum", [64, 1], F32)
        L_tm = p.sbuf("mb_ltm", [128, 32], F32)
        DT_tm = p.sbuf("mb_dttm", [128, 32], F32)
        EXL = p.sbuf("mb_exl", [128, 32], F32)
        EML = p.sbuf("mb_eml", [128, 1], F32)
        ES = p.sbuf("mb_es", [128, 1], F32)
        LT2 = p.sbuf("mb_lt2", [128, 2], F32)
        MT2 = p.sbuf("mb_mt2", [128, 2], F32)
        TL = p.sbuf("mb_tl", [2, 128], F32)
        TR = p.sbuf("mb_tr", [2, 128], F32)
        LSR = p.sbuf("mb_lsr", [1, 32], F32)
        LSB = p.sbuf("mb_lsb", [128, 1], F32)
        ME = p.sbuf("mb_me", [128, 128], F32)
        eerow = p.sbuf("mb_eer", [1, 32], F32)
        drow = p.sbuf("mb_drow", [1, 32], F32)
        Btm = p.sbuf("mb_btm", [128, 16], BF16)
        Bsc = p.sbuf("mb_bsc", [128, 16], BF16)
        xstm = p.sbuf("mb_xstm", [128, 64], BF16)
        Mt = p.sbuf("mb_mt", [128, 128], F32)
        Mbf = p.sbuf("mb_mbf", [128, 128], BF16)
        hloc = p.sbuf("mb_hloc", [16, 2048], F32)
        hlocb = p.sbuf("mb_hlocb", [16, 2048], BF16)
        eesb = p.sbuf("mb_eesb", [16, 1], F32)
        yfin = p.sbuf("mb_yfin", [128, 64], F32)
        xsd = p.sbuf("mb_xsd", [128, 64], BF16)
        Y = [p.sbuf("mb_y", [128, BLK], BF16) for _ in range(16)]
        wslab = [p.sbuf("mb_ws", [128, 1024], BF16) for _ in range(2)]
        sil = p.sbuf("mb_sil", [128, BLK], F32)
        silb = p.sbuf("mb_silb", [16, BLK], F32)
        p.vmemset(ONES_BK, 1.0, tag=tag + "_o1")
        p.vmemset((LT2, (slice(None), slice(1, 2))), 1.0, tag=tag + "_o2")
        p.vmemset((MT2, (slice(None), slice(0, 1))), -1.0, tag=tag + "_o3")
        p.act((ACOL, (slice(32, 64), slice(0, 1))),
              (C.cols, (slice(32, 64), slice(C.cols_index[f"mb{dirn}_alog"][0], C.cols_index[f"mb{dirn}_alog"][0] + 1))),
              AF.Exp, tag=tag + "_ae")
        p.vts((ACOL, (slice(32, 64), slice(0, 1))), (ACOL, (slice(32, 64), slice(0, 1))), -1.0, ALU.mult, tag=tag + "_an")
        for inst in range(4):
            c0 = inst * BLK
            cs0 = slice(c0, c0 + BLK)
            hs6 = slice(inst * 6, inst * 6 + 6)
            with p.scope():
                ps = [p.psum("mbA_ps", [128, BLK]) for _ in range(2)]
                ps_h = p.psum("mbA_psh", [128, 6])
                for m in range(17):
                    slab = wslab[m % 2]
                    p.dma("pool", slab, (wipb, (slice(m * 128, (m + 1) * 128), slice(None))), tag=tag + "_wd")
                    rows = 128 if m < 16 else 64
                    for kc in range(NCH):
                        p.mm((ps[m % 2], (slice(0, rows), slice(None))),
                             (slab, (slice(None), slice(kc * 128, kc * 128 + rows))),
                             (C.Z[kc], (slice(None), cs0)),
                             start=(kc == 0), stop=(kc == NCH - 1), tag=tag + "_mm")
                    for kc in range(NCH):
                        p.mm((ps_h, (slice(0, rows), slice(None))),
                             (slab, (slice(None), slice(kc * 128, kc * 128 + rows))),
                             (HZ[kc], (slice(None), hs6)),
                             start=(kc == 0), stop=(kc == NCH - 1), tag=tag + "_mh")
                    if m < 16:
                        p.vcopy((EXT[m], (slice(None), slice(3, 3 + BLK))), ps[m % 2], tag=tag + "_xe")
                        p.vcopy((EXT[m], (slice(None), slice(0, 3))), (ps_h, (slice(None), slice(0, 3))), tag=tag + "_xh1")
                        p.vcopy((EXT[m], (slice(None), slice(3 + BLK, 6 + BLK))), (ps_h, (slice(None), slice(3, 6))), tag=tag + "_xh2")
                    else:
                        p.vcopy((EXT[16], (slice(0, 64), slice(3, 3 + BLK))), (ps[m % 2], (slice(0, 64), slice(None))), tag=tag + "_xe2")
                        p.vcopy((EXT[16], (slice(0, 64), slice(0, 3))), (ps_h, (slice(0, 64), slice(0, 3))), tag=tag + "_xh3")
                        p.vcopy((EXT[16], (slice(0, 64), slice(3 + BLK, 6 + BLK))), (ps_h, (slice(0, 64), slice(3, 6))), tag=tag + "_xh4")
            p.dma("pool", Cext, (EXT[16], (slice(16, 32), slice(None))), tag=tag + "_cext")
            def conv(dst, src, wname, nrows, tag2="cv"):
                rs = slice(0, nrows)
                for j in range(4):
                    off = j if fwd else 6 - j
                    ss = (rs, slice(off, off + BLK))
                    wcap = (C.cols, (rs, slice(C.cols_index[wname][0] + j, C.cols_index[wname][0] + j + 1)))
                    if j == 0:
                        p.vts(dst, (src, ss), wcap, ALU.mult, tag=tag + tag2 + "0")
                    else:
                        p.vstt(dst, (src, ss), wcap, dst, ALU.mult, ALU.add, tag=tag + tag2 + "1")
            for c2 in range(16):
                conv(sil, EXT[c2], f"mb{dirn}_cw{c2}", 128, tag2=f"_cv{c2}")
                p.act(XS[c2], sil, AF.Silu, bias=C.col(f"mb{dirn}_cb", c2), tag=tag + "_sl")
            conv(silb, EXT[16], f"mb{dirn}_cwB", 16, tag2="_cvB")
            p.act(Bc, silb, AF.Silu, bias=(C.cols, (slice(0, 16), slice(C.cols_index[f"mb{dirn}_cbB"][0], C.cols_index[f"mb{dirn}_cbB"][0] + 1))), tag=tag + "_slB")
            conv(silb, (Cext, (slice(0, 16), slice(0, BLK + 6))) if False else Cext, f"mb{dirn}_cwC", 16, tag2="_cvC")
            p.act(Cc, silb, AF.Silu, bias=(C.cols, (slice(0, 16), slice(C.cols_index[f"mb{dirn}_cbC"][0], C.cols_index[f"mb{dirn}_cbC"][0] + 1))), tag=tag + "_slC")
            p.dma("sp", (ccd, (slice(None), cs0)), Cc, tag=tag + "_cco")
            d32 = (slice(32, 64), slice(None))
            ext_dt = (EXT[16], (slice(32, 64), slice(3, 3 + BLK)))
            dtb = (C.cols, (slice(32, 64), slice(C.cols_index[f"mb{dirn}_dtb"][0], C.cols_index[f"mb{dirn}_dtb"][0] + 1)))
            p.vts((DT, d32), ext_dt, dtb, ALU.add, tag=tag + "_d0")
            p.raw("act", lambda e, o=DTA.h[32:64, :], i=DT.h[32:64, :]: e.activation(o, i, AF.Abs), [DT.key], [DTA.key], tag + "_d1")
            p.raw("act", lambda e, o=LL.h[32:64, :], i=DTA.h[32:64, :]: e.activation(o, i, AF.Exp, scale=-1.0), [DTA.key], [LL.key], tag + "_d2")
            p.raw("act", lambda e, o=DTA.h[32:64, :], i=LL.h[32:64, :]: e.activation(o, i, AF.Ln, bias=1.0), [LL.key], [DTA.key], tag + "_d3")
            p.vts((LL, d32), (DT, d32), 0.0, ALU.max, tag=tag + "_d4")
            p.vtt((DT, d32), (LL, d32), (DTA, d32), ALU.add, tag=tag + "_d5")
            p.vts((DTA, d32), (DT, d32), (ACOL, (slice(32, 64), slice(0, 1))), ALU.mult, tag=tag + "_d6")
            p.vmemset(hloc, 0.0, tag=tag + "_h0")
            p.vmemset(hlocb, 0.0, tag=tag + "_h0b")
            p.vmemset(drow, 1.0, tag=tag + "_dr0")
            chunks = range(4) if fwd else range(3, -1, -1)
            with p.scope():
                ps_g = p.psum("mbA_psg", [128, 128])
                ps_tr = p.psum("mbA_pstr", [128, 128])
                ps_tr2 = p.psum("mbA_pstr2", [128, 64], BF16)
                ps_y = p.psum("mbA_psy", [128, 64])
                ps_y2 = p.psum("mbA_psy2", [128, 64])
                ps_hu = p.psum("mbA_pshu", [16, 64])
                ps_dif = p.psum("mbA_psdif", [128, 128])
                ps_yt = p.psum("mbA_psyt", [128, 128])
                for ch in chunks:
                    ts = slice(ch * 128, (ch + 1) * 128)
                    p.raw("dve", lambda e, o=LL.h[32:64, ts], a=ONES_BK.h[32:64, ts], b=DTA.h[32:64, ts]:
                          e.tensor_tensor_scan(o, a, b, 0.0, ALU.mult, ALU.add), [ONES_BK.key, DTA.key], [LL.key], tag + "_sc")
                    p.raw("dve", lambda e, o=LSUM.h[32:64, :], i=DTA.h[32:64, ts]:
                          e.tensor_reduce(o, i, AX.X, ALU.add), [DTA.key], [LSUM.key], tag + "_ls")
                    if not fwd:
                        p.raw("dve", lambda e, o=LL.h[32:64, ts], a=LL.h[32:64, ts], b=DTA.h[32:64, ts]:
                              e.tensor_tensor(o, a, b, ALU.subtract), [LL.key, DTA.key], [LL.key], tag + "_lb1")
                        p.raw("dve", lambda e, o=LL.h[32:64, ts], i=LL.h[32:64, ts], s=LSUM.h[32:64, :]:
                              e.tensor_scalar(o, i, -1.0, s, ALU.mult, ALU.add), [LL.key, LSUM.key], [LL.key], tag + "_lb2")
                    p.transpose((ps_tr, (slice(None), slice(0, 32))), (LL, (slice(32, 64), ts)), C.cst("eye", rows=slice(32, 64), cols=slice(32, 64)), tag=tag + "_lt")
                    p.scopy(L_tm, (ps_tr, (slice(None), slice(0, 32))), tag=tag + "_lte")
                    p.transpose((ps_tr, (slice(None), slice(0, 32))), (DT, (slice(32, 64), ts)), C.cst("eye", rows=slice(32, 64), cols=slice(32, 64)), tag=tag + "_dt")
                    p.scopy(DT_tm, (ps_tr, (slice(None), slice(0, 32))), tag=tag + "_dte")
                    p.act(EXL, L_tm, AF.Exp, tag=tag + "_exl")
                    p.dma("act", (etd, (slice(c0 + ch * 128, c0 + (ch + 1) * 128), slice(None))), EXL, tag=tag + "_eto")
                    p.transpose((ps_tr, (slice(0, 1), slice(0, 32))),
                                (LSUM, (slice(32, 64), slice(0, 1))), C.cst("eye", rows=slice(32, 64), cols=slice(32, 64)), tag=tag + "_et")
                    p.scopy(LSR, (ps_tr, (slice(0, 1), slice(0, 32))), tag=tag + "_lsr")
                    p.act(eerow, LSR, AF.Exp, tag=tag + "_ee")
                    p.dma("act", (eed, (slice(inst * 4 + ch, inst * 4 + ch + 1), slice(None))), eerow, tag=tag + "_eeo")
                    p.vtt(drow, drow, eerow, ALU.mult, tag=tag + "_dacc")
                    p.transpose((ps_tr2, (slice(None), slice(0, 16))), (Bc, (slice(0, 16), ts)), C.cst("eye", bf=True, rows=slice(0, 16), cols=slice(0, 16)), tag=tag + "_bt")
                    p.scopy((Btm, (slice(None), slice(0, 16))), (ps_tr2, (slice(None), slice(0, 16))), tag=tag + "_bte")
                    p.mm(ps_g, (Bc, (slice(0, 16), ts)), (Cc, (slice(0, 16), ts)), tag=tag + "_g")
                    for h in range(32):
                        oc, off = divmod(h, 2)
                        hs = slice(off * 64, (off + 1) * 64)
                        hcol = slice(h * 64, (h + 1) * 64)
                        lcol = (L_tm, (slice(None), slice(h, h + 1)))
                        p.vcopy((LT2, (slice(None), slice(0, 1))), lcol, tag=tag + "_lc1")
                        p.vcopy((MT2, (slice(None), slice(1, 2))), lcol, tag=tag + "_lc2")
                        p.transpose((ps_tr, (slice(0, 2), slice(0, 128))), LT2, C.cst("eye"), tag=tag + "_t1")
                        p.scopy(TL, (ps_tr, (slice(0, 2), slice(0, 128))), tag=tag + "_t1e")
                        p.transpose((ps_tr, (slice(0, 2), slice(0, 128))), MT2, C.cst("eye"), tag=tag + "_t2")
                        p.scopy(TR, (ps_tr, (slice(0, 2), slice(0, 128))), tag=tag + "_t2e")
                        p.mm(ps_dif, TL, TR, tag=tag + "_dif")
                        p.vtt(Mt, ps_dif, C.cst("neg_inc") if fwd else C.cst("neg_inc_rev"), ALU.add, tag=tag + "_msk")
                        p.act(ME, Mt, AF.Exp, tag=tag + "_mex")
                        p.vtt(Mt, ME, ps_g, ALU.mult, tag=tag + "_mg")
                        p.vts(Mbf, Mt, (DT_tm, (slice(None), slice(h, h + 1))), ALU.mult, tag=tag + "_m2")
                        eyeb = C.cst("eye", bf=True, rows=hs, cols=slice(0, 64)) if off == 0 else C.cst("eye2", bf=True, rows=hs, cols=slice(0, 64))
                        p.transpose(ps_tr2, (XS[oc], (hs, ts)), eyeb, tag=tag + "_xt")
                        p.scopy(xstm, ps_tr2, tag=tag + "_xte")
                        p.mm(ps_y, Mbf, xstm, tag=tag + "_y1")
                        p.mm(ps_y2, (Cc, (slice(0, 16), ts)), (hlocb, (slice(None), hcol)), tag=tag + "_y2")
                        p.vts(xsd, xstm, C.col(f"mb{dirn}_dp", h), ALU.mult, tag=tag + "_dpx")
                        p.act(EML, lcol, AF.Exp, tag=tag + "_ep")
                        p.vstt(yfin, ps_y2, EML, xsd, ALU.mult, ALU.add, tag=tag + "_yf")
                        p.vtt(yfin, yfin, ps_y, ALU.add, tag=tag + "_yf2")
                        p.mm((ps_tr, (slice(None), slice(32, 33))), C.ones_row(), (LSR, (slice(0, 1), slice(h, h + 1))), tag=tag + "_lsb")
                        p.scopy(LSB, (ps_tr, (slice(None), slice(32, 33))), tag=tag + "_lse")
                        p.act(ES, lcol, AF.Exp, scale=-1.0, bias=LSB, tag=tag + "_es")
                        p.vts(Bsc, Btm, (DT_tm, (slice(None), slice(h, h + 1))), ALU.mult, tag=tag + "_bs1")
                        p.vts(Bsc, Bsc, ES, ALU.mult, tag=tag + "_bs2")
                        p.mm(ps_hu, Bsc, xstm, tag=tag + "_hu")
                        p.mm((ps_tr, (slice(0, 16), slice(33, 34))), C.ones_row(16), (eerow, (slice(0, 1), slice(h, h + 1))), tag=tag + "_eeb")
                        p.scopy(eesb, (ps_tr, (slice(0, 16), slice(33, 34))), tag=tag + "_eee")
                        p.vts((hloc, (slice(None), hcol)), (hloc, (slice(None), hcol)), eesb, ALU.mult, tag=tag + "_hd")
                        p.vtt((hloc, (slice(None), hcol)), (hloc, (slice(None), hcol)), ps_hu, ALU.add, tag=tag + "_ha")
                        p.vcopy((hlocb, (slice(None), hcol)), (hloc, (slice(None), hcol)), tag=tag + "_hb")
                        p.transpose((ps_yt, (slice(0, 64), slice(0, 128))), yfin, C.cst("eye"), tag=tag + "_yt")
                        p.scopy((Y[oc], (hs, ts)), (ps_yt, (slice(0, 64), slice(0, 128))), tag=tag + "_ye")
            for c2 in range(16):
                p.dma("sp", (yd, (slice(c2 * 128, (c2 + 1) * 128), cs0)), Y[c2], tag=tag + "_yo")
            p.dma("sp", (stated, (slice(inst * 16, (inst + 1) * 16), slice(None))), hloc, tag=tag + "_so")
            p.dma("sp", (decd, (slice(inst, inst + 1), slice(None))), drow, tag=tag + "_do")


def emit_mamba_passB(p, C, dirn, yd, ccd, etd, eed, hacc_scratch, tag=None):
    tag = tag or f"mb{dirn}B"
    fwd = dirn == "F"
    allst = p.dram_in(f"mb{dirn}_allst", [16 * 128, 1024], F32)
    decrepT = p.dram_in(f"mb{dirn}_decrepT", [128, 16], F32)
    wipz = p.dram_in(f"mb{dirn}_ipzT", [16 * 128, 1024], F32)
    wopw = p.dram_in(f"mb{dirn}_opw", [1024, 2048], F32)
    wlo = p.dram_in("mb_loT", [2048, 1024], F32) if not hasattr(C, "wloT") else C.wloT
    C.wloT = wlo
    wpT = p.dram_tmp("wpT", [16 * 128, 1024], BF16)
    r0lo = 0 if fwd else 1024
    with p.scope():
        ops_ = [p.sbuf("mbW_op", [128, 2048], BF16) for _ in range(NCH)]
        wls_ = [p.sbuf("mbW_wl", [128, 1024], BF16) for _ in range(NCH)]
        stg = p.sbuf("mbW_st", [128, 1024], BF16)
        for kc in range(NCH):
            p.dma("pool", ops_[kc], (wopw, (slice(kc * 128, (kc + 1) * 128), slice(None))), tag=tag + "_pw1")
            p.dma("pool", wls_[kc], (wlo, (slice(r0lo + kc * 128, r0lo + (kc + 1) * 128), slice(None))), tag=tag + "_pw2")
        with p.scope():
            ps_w = [p.psum("mbW_ps", [128, 512]) for _ in range(2)]
            for ych in range(16):
                for nq in range(2):
                    for kc in range(NCH):
                        p.mm(ps_w[nq], (ops_[kc], (slice(None), slice(ych * 128, (ych + 1) * 128))),
                             (wls_[kc], (slice(None), slice(nq * 512, (nq + 1) * 512))),
                             start=(kc == 0), stop=(kc == NCH - 1), tag=tag + "_pwm")
                    p.vcopy((stg, (slice(None), slice(nq * 512, (nq + 1) * 512))), ps_w[nq], tag=tag + "_pwe")
                p.dma("sp", (wpT, (slice(ych * 128, (ych + 1) * 128), slice(None))), stg, tag=tag + "_pwo")
    with p.scope():
        Ex32 = p.sbuf("mbB_ex32", [32, 2048], BF16)
        with p.scope():
            exi = p.sbuf("mbB_exi", [32, 2048], I32)
            exp_ = p.sbuf("mbB_exp", [32, 1], I32)
            exf = p.sbuf("mbB_exf", [32, 2048], F32)
            exc = p.sbuf("mbB_exc", [32, 1], F32)
            p.iota(exi, pattern=[[1, 32], [0, 64]], base=0, channel_multiplier=0, tag=tag + "_i1")
            p.iota(exp_, pattern=[[0, 1]], base=0, channel_multiplier=1, tag=tag + "_i2")
            p.vcopy(exf, exi, tag=tag + "_i3")
            p.vcopy(exc, exp_, tag=tag + "_i4")
            p.vts(Ex32, exf, exc, ALU.is_equal, tag=tag + "_i5")
        hacc = p.sbuf("mbB_hacc", [128, 1024], F32)
        slab = [p.sbuf("mbB_sl", [128, 1024], F32) for _ in range(2)]
        drT = p.sbuf("mbB_drt", [128, 16], F32)
        dm = p.sbuf("mbB_dm", [128, 1], F32)
        p.dma("sp", drT, decrepT, tag=tag + "_dr")
        p.vmemset(hacc, 0.0, tag=tag + "_h0")
        for bi in range(16):
            p.dma("pool", slab[bi % 2], (allst, (slice(bi * 128, (bi + 1) * 128), slice(None))), tag=tag + "_sl")
            p.vts(dm, (drT, (slice(None), slice(bi, bi + 1))), C.col(f"mb{dirn}_ind", bi), ALU.mult,
                  C.col(f"mb{dirn}_ind1m", bi), ALU.add, tag=tag + "_dm")
            p.vts(hacc, hacc, dm, ALU.mult, tag=tag + "_hm")
            p.vstt(hacc, slab[bi % 2], C.col(f"mb{dirn}_ind", bi), hacc, ALU.mult, ALU.add, tag=tag + "_hs")
        p.dma("sp", hacc_scratch, hacc, tag=tag + "_hb1")
        hin = p.sbuf("mbB_hin", [16, 2048], F32)
        hinb = p.sbuf("mbB_hinb", [16, 2048], BF16)
        hv = hacc_scratch.h.rearrange("(i h) (n p) -> i n h p", i=4, n=16)
        Y = [p.sbuf("mbB_y", [128, BLK], BF16) for _ in range(16)]
        zs = [p.sbuf("mbB_z", [128, BLK], BF16) for _ in range(16)]
        yg = Y
        Ccb = p.sbuf("mbB_cc", [16, BLK], BF16)
        EXL = p.sbuf("mbB_exl", [128, 32], F32)
        eer = p.sbuf("mbB_eer", [1, 32], F32)
        eec = p.sbuf("mbB_eec", [32, 1], F32)
        exs = p.sbuf("mbB_exs", [32, 2048], BF16)
        yx = p.sbuf("mbB_yx", [128, 64], F32)
        sq = p.sbuf("mbB_sq", [128, BLK], F32)
        rrow = p.sbuf("mbB_rr", [1, BLK], F32)
        tmpf = p.sbuf("mbB_tf", [128, BLK], F32)
        wslab = [p.sbuf("mbB_ws", [128, 2048], BF16) for _ in range(2)]
        for inst in range(4):
            c0 = inst * BLK
            cs0 = slice(c0, c0 + BLK)
            for c2 in range(16):
                p.dma("sp", Y[c2], (yd, (slice(c2 * 128, (c2 + 1) * 128), cs0)), tag=tag + "_yi")
            p.dma("pool", Ccb, (ccd, (slice(None), cs0)), tag=tag + "_cci")
            p.dma("sp", hin, (hv[inst:inst + 1], hacc_scratch.key), tag=tag + "_hb2")
            p.vcopy(hinb, hin, tag=tag + "_hc")
            chunks = range(4) if fwd else range(3, -1, -1)
            with p.scope():
                ps_y = p.psum("mbB_psy", [128, 64])
                ps_yt = p.psum("mbB_psyt", [128, 128])
                ps_e = p.psum("mbB_pse", [16, 512])
                ps_t1 = p.psum("mbB_pst1", [32, 32])
                for ch in chunks:
                    ts = slice(ch * 128, (ch + 1) * 128)
                    p.dma("act", EXL, (etd, (slice(c0 + ch * 128, c0 + (ch + 1) * 128), slice(None))), tag=tag + "_exi")
                    p.dma("act", eer, (eed, (slice(inst * 4 + ch, inst * 4 + ch + 1), slice(None))), tag=tag + "_eei")
                    for h in range(32):
                        oc, off = divmod(h, 2)
                        hs = slice(off * 64, (off + 1) * 64)
                        hcol = slice(h * 64, (h + 1) * 64)
                        p.mm(ps_y, (Ccb, (slice(None), ts)), (hinb, (slice(None), hcol)), tag=tag + "_ym")
                        p.vts(yx, ps_y, (EXL, (slice(None), slice(h, h + 1))), ALU.mult, tag=tag + "_ysc")
                        p.transpose((ps_yt, (slice(0, 64), slice(0, 128))), yx, C.cst("eye"), tag=tag + "_yt")
                        p.vtt((Y[oc], (hs, ts)), (ps_yt, (slice(0, 64), slice(0, 128))), (Y[oc], (hs, ts)), ALU.add, tag=tag + "_ya")
                    p.transpose((ps_t1, (slice(0, 32), slice(0, 1))), eer,
                                C.cst("eye", rows=slice(0, 1), cols=slice(0, 1)), tag=tag + "_et")
                    p.scopy(eec, (ps_t1, (slice(0, 32), slice(0, 1))), tag=tag + "_ete")
                    p.vts(exs, Ex32, eec, ALU.mult, tag=tag + "_exm")
                    for qn in range(4):
                        ns = slice(qn * 512, (qn + 1) * 512)
                        p.mm(ps_e, C.cst("ones_blk", bf=True, rows=slice(0, 32), cols=slice(0, 16)),
                             (exs, (slice(None), ns)), tag=tag + "_eb")
                        p.vtt((hin, (slice(None), ns)), (hin, (slice(None), ns)), ps_e, ALU.mult, tag=tag + "_he")
                    p.vcopy(hinb, hin, tag=tag + "_hc2")
            with p.scope():
                ps = [p.psum("mbB_psz", [128, BLK]) for _ in range(2)]
                for m in range(16):
                    p.dma("pool", (wslab[m % 2], (slice(None), slice(0, 1024))),
                          (wipz, (slice(m * 128, (m + 1) * 128), slice(None))), tag=tag + "_zw")
                    for kc in range(NCH):
                        p.mm(ps[m % 2], (wslab[m % 2], (slice(None), slice(kc * 128, (kc + 1) * 128))),
                             (C.Z[kc], (slice(None), cs0)),
                             start=(kc == 0), stop=(kc == NCH - 1), tag=tag + "_zm")
                    p.act(zs[m], ps[m % 2], AF.Silu, tag=tag + "_zs")
            if os.environ.get("MB_DEBUG") and dirn == "F" and inst == 0:
                dbg1 = p.dram_out("dbg_zs", [2048, BLK], BF16)
                dbg2 = p.dram_out("dbg_y", [2048, BLK], BF16)
                for c2 in range(16):
                    p.dma("sp", (dbg1, (slice(c2 * 128, (c2 + 1) * 128), slice(None))), zs[c2], tag="dbgz")
                    p.dma("sp", (dbg2, (slice(c2 * 128, (c2 + 1) * 128), slice(None))), Y[c2], tag="dbgy")
            with p.scope():
                ps_r = p.psum("mbB_psr", [1, BLK])
                ps_bc = p.psum("mbB_psbc", [128, BLK])
                for c2 in range(16):
                    p.vtt(yg[c2], Y[c2], zs[c2], ALU.mult, tag=tag + "_g1")
                    p.act(sq, yg[c2], AF.Square, tag=tag + "_g2")
                    p.mm(ps_r, C.cst("ones_col"), sq, start=(c2 == 0), stop=(c2 == 15), tag=tag + "_g3")
                p.vts(rrow, ps_r, 1.0 / 2048.0, ALU.mult, tag=tag + "_g4")
                p.act(rrow, rrow, AF.Sqrt, bias=C.col1("eps"), tag=tag + "_g5")
                p.vrecip(rrow, rrow, tag=tag + "_g6")
                p.mm(ps_bc, C.ones_row(), rrow, tag=tag + "_g7")
                for c2 in range(16):
                    p.vtt(tmpf, yg[c2], ps_bc, ALU.mult, tag=tag + "_g8")
                    p.vts(yg[c2], tmpf, C.col(f"mb{dirn}_nw", c2), ALU.mult, tag=tag + "_g9")
            if os.environ.get("MB_DEBUG") and dirn == "F" and inst == 0:
                dbg3 = p.dram_out("dbg_yg", [2048, BLK], BF16)
                for c2 in range(16):
                    p.dma("sp", (dbg3, (slice(c2 * 128, (c2 + 1) * 128), slice(None))), yg[c2], tag="dbgg")
            with p.scope():
                ps_o = [p.psum("mbB_pso", [128, BLK]) for _ in range(NCH)]
                for kc in range(16):
                    p.dma("pool", (wslab[kc % 2], (slice(None), slice(0, 1024))),
                          (wpT, (slice(kc * 128, (kc + 1) * 128), slice(None))), tag=tag + "_wpd")
                    for tsub in range(4):
                        for oc in range(NCH):
                            p.mm((ps_o[oc], (slice(None), slice(tsub * 128, (tsub + 1) * 128))),
                                 (wslab[kc % 2], (slice(None), slice(oc * 128, (oc + 1) * 128))),
                                 (yg[kc], (slice(None), slice(tsub * 128, (tsub + 1) * 128))),
                                 start=(kc == 0 and tsub == 0), stop=(kc == 15 and tsub == 3), tag=tag + "_om")
                for oc in range(NCH):
                    p.vstt((C.X[oc], (slice(None), cs0)), ps_o[oc], C.col("mb_ls", oc),
                           (C.X[oc], (slice(None), cs0)), ALU.mult, ALU.add, tag=tag + "_res")



def host_mamba_cols(dirn, in_proj_w, conv_w, conv_b, dt_bias, A_log, Dp, norm_w):
    cols = {}
    for c2 in range(16):
        w = np.zeros((4 * 128,), np.float32)
        for j in range(4):
            w[j * 128:(j + 1) * 128] = conv_w[c2 * 128:(c2 + 1) * 128, 0, j]
        cols[f"mb{dirn}_cw{c2}"] = w
    for nm, ch0 in (("B", 2048), ("C", 2064)):
        w = np.zeros((4 * 128,), np.float32)
        for j in range(4):
            w[j * 128:j * 128 + 16] = conv_w[ch0:ch0 + 16, 0, j]
        cols[f"mb{dirn}_cw{nm}"] = w
        bb = np.zeros(128, np.float32)
        bb[:16] = conv_b[ch0:ch0 + 16]
        cols[f"mb{dirn}_cb{nm}"] = bb
    cb = np.zeros(16 * 128, np.float32)
    cb[:2048] = conv_b[:2048]
    cols[f"mb{dirn}_cb"] = cb
    v = np.zeros(128, np.float32)
    v[32:64] = dt_bias
    cols[f"mb{dirn}_dtb"] = v
    v = np.zeros(128, np.float32)
    v[32:64] = A_log
    cols[f"mb{dirn}_alog"] = v
    dp = np.zeros(32 * 128, np.float32)
    for h in range(32):
        dp[h * 128:(h + 1) * 128] = Dp[h]
    cols[f"mb{dirn}_dp"] = dp
    cols[f"mb{dirn}_nw"] = norm_w
    return cols


def host_mamba_weights(dirn, in_proj_w, out_proj_w):
    wz = host_tile_wT(in_proj_w[:2048])
    wb = np.zeros((17 * 128, 1024), np.float32)
    for m in range(17):
        rows = 128 if m < 16 else 64
        blk = in_proj_w[2048 + m * 128: 2048 + m * 128 + rows]
        for kc in range(8):
            wb[m * 128:(m + 1) * 128, kc * 128:kc * 128 + rows] = blk[:, kc * 128:(kc + 1) * 128].T
    return {f"mb{dirn}_ipzT": wz, f"mb{dirn}_ipbT": np.ascontiguousarray(wb),
            f"mb{dirn}_opw": np.ascontiguousarray(out_proj_w)}


def host_mamba_ind(core):
    cols = {}
    for dirn in ("F", "B"):
        order = list(range(NBLK)) if dirn == "F" else list(range(NBLK - 1, -1, -1))
        v = np.zeros(NBLK * 128, np.float32)
        for bi in range(NBLK):
            for i, (bt, blk) in enumerate(inst_map(core)):
                pos_of_blk = order.index(blk)
                if bi < pos_of_blk:
                    v[bi * 128 + i * 32: bi * 128 + (i + 1) * 32] = 1.0
        cols[f"mb{dirn}_ind"] = v
        cols[f"mb{dirn}_ind1m"] = 1.0 - v
    return cols


def emit_l1attnb_A(p, C, halod, exports, tag="mbA"):
    emit_ln(p, C, tag=tag + "_ln")
    for dirn in ("F", "B"):
        emit_mamba_passA(p, C, dirn, halod, *exports[dirn])


def emit_l1attnb_B(p, C, imports, hacc_scratch, ln=True, tag="mbB"):
    if ln:
        emit_ln(p, C, tag=tag + "_ln")
    for dirn in ("F", "B"):
        emit_mamba_passB(p, C, dirn, *imports[dirn], hacc_scratch)
    for oc in range(NCH):
        p.vtt(C.X[oc], C.X[oc], C.Z[oc], ALU.add, tag=tag + "_lnres")


def _mk_ctx(p, core, cols_idx, consts_idx, ncols, nconsts):
    return Ctx(p, core, cols_idx, consts_idx, ncols, nconsts)


def kernel(**inp):
    consts_blob, consts_idx = host_consts()
    wm_l0a, cols_l0a = host_l0a_weights(
        inp["L0_attn_a_qkv_w"], inp["L0_attn_a_qkv_b"], inp["L0_attn_a_gate_w"],
        inp["L0_attn_a_gate_b"], inp["L0_attn_a_proj_w"], inp["L0_attn_a_proj_b"],
        inp["L0_attn_a_ln_g"], inp["L0_attn_a_ln_b"])
    wm_moe = emit_l0ffn_weights_host(inp["L0_ffn_w12"], inp["L0_ffn_w3"], inp["L0_ffn_gate_w"])
    wm_l0b = {"l0b_wqkv": np.ascontiguousarray(np.concatenate(
        [inp["L0_attn_b_Wq"], inp["L0_attn_b_Wkv"]], axis=1)), "l0b_wo": inp["L0_attn_b_Wo"]}
    cols_l0b = {"l0b_g": inp["L0_attn_b_ln_gamma"], "l0b_b": inp["L0_attn_b_ln_beta"]}
    wm_l1a = {"l1a_qkvt": host_tile_wT(np.vstack([inp["L1_attn_a_Wq"], inp["L1_attn_a_Wk"], inp["L1_attn_a_Wv"]])),
              "l1a_wtqT": np.ascontiguousarray(inp["L1_attn_a_Wtq"].T),
              "l1a_wot": np.ascontiguousarray(inp["L1_attn_a_Wo"].T),
              "l1a_wfqT": np.ascontiguousarray(inp["L1_attn_a_Wfq"].T)}
    cols_l1a = {"l1a_bq": inp["L1_attn_a_bq"], "l1a_bk": inp["L1_attn_a_bk"],
                "l1a_bv": inp["L1_attn_a_bv"],
                "l1a_btq0": np.full(128, inp["L1_attn_a_btq"][0], np.float32),
                "l1a_btq1": np.full(128, inp["L1_attn_a_btq"][1], np.float32),
                "l1a_bo": inp["L1_attn_a_bo"], "l1a_bfq": inp["L1_attn_a_bfq"]}
    wm_l1f = {"l1f_ld": inp["L1_ffn_lora_down"], "l1f_lu": inp["L1_ffn_lora_up"],
              "l1f_upT": np.ascontiguousarray(inp["L1_ffn_up_aux"].T),
              "l1f_da": inp["L1_ffn_down_aux"],
              "l1f_wT": np.ascontiguousarray(inp["L1_ffn_W"].T)}
    cols_l1f = {"l1f_b": inp["L1_ffn_b"]}
    cols_mb = {"mb_ls": inp["L1_attn_b_layer_scale"]}
    wm_mb = {"mb_loT": np.ascontiguousarray(inp["L1_attn_b_layer_out_proj_w"].T)}
    for dirn, pre in (("F", "fwd"), ("B", "bwd")):
        cols_mb.update(host_mamba_cols(dirn, inp[f"L1_attn_b_{pre}_in_proj_w"],
                                       inp[f"L1_attn_b_{pre}_conv_w"], inp[f"L1_attn_b_{pre}_conv_b"],
                                       inp[f"L1_attn_b_{pre}_dt_bias"], inp[f"L1_attn_b_{pre}_A_log"],
                                       inp[f"L1_attn_b_{pre}_Dp"], inp[f"L1_attn_b_{pre}_norm_w"]))
        wm_mb.update(host_mamba_weights(dirn, inp[f"L1_attn_b_{pre}_in_proj_w"],
                                        inp[f"L1_attn_b_{pre}_out_proj_w"]))
    all_w = {**wm_l0a, **wm_moe, **wm_l0b, **wm_l1a, **wm_l1f, **wm_mb}
    NC_CONSTS = consts_blob.shape[1]

    l0a_inds = []
    for c in range(NCORES):
        l0a_ind = np.zeros(4 * 2 * NBLK * 128, np.float32)
        for i, (bt, blk) in enumerate(inst_map(c)):
            for bp in range(2 * NBLK):
                if bp // NBLK == bt and (bp % NBLK) < blk:
                    l0a_ind[(i * 2 * NBLK + bp) * 128:(i * 2 * NBLK + bp + 1) * 128] = 1.0
        l0a_inds.append(l0a_ind)

    def mkcols(base, per_core=None):
        blobs, idx = [], None
        for c in range(NCORES):
            cc = dict(base)
            if per_core is not None:
                cc.update(per_core(c))
            blob, idx = host_pack_cols(cc)
            blobs.append(blob)
        return blobs, idx, blobs[0].shape[1]

    xs = host_shard_x(np.asarray(inp["L0_attn_a_x"], dtype=np.float32))

    cols_blobs = [None] * NCORES

    def base_maps():
        return [{"cols": cols_blobs[c], "consts": consts_blob} for c in range(NCORES)]

    def run(nc, maps):
        res = run_bass_kernel_spmd(nc, maps, core_ids=list(range(NCORES)))
        return res.results

    def filt(nc, maps):
        names = set()
        for a in nc.m.functions[0].allocations:
            try:
                if a.kind == "ExternalInput":
                    names.add(a.memorylocations[0].name)
            except Exception:
                pass
        return [{k: v for k, v in m.items() if k in names} for m in maps]

    cols_blobs, cols_idx, NC_COLS = mkcols(cols_l0a)
    p = P()
    xin = p.dram_in("xin", [D, TOK], F32)
    fq = p.dram_out("fq", [D, TOK], BF16)
    fk = p.dram_out("fk", [D, TOK], BF16)
    fv = p.dram_out("fv", [D, TOK], BF16)
    states = p.dram_out("states", [4 * 128, 8 * 65], F32)
    C = _mk_ctx(p, 0, cols_idx, consts_idx, NC_COLS, NC_CONSTS)
    emit_load_stream(p, C, xin)
    emit_l0attna_passA(p, C, fq, fk, fv, states)
    nc1 = p.build()
    maps = base_maps()
    for c in range(NCORES):
        maps[c].update({"xin": xs[c], **all_w})
    r1 = run(nc1, filt(nc1, maps))

    allstates = np.zeros((2 * NBLK * 128, 8 * 65), np.float32)
    for c in range(NCORES):
        for i, (bt, blk) in enumerate(inst_map(c)):
            allstates[(bt * NBLK + blk) * 128:(bt * NBLK + blk + 1) * 128] = \
                r1[c]["states"][i * 128:(i + 1) * 128]

    cols_blobs, cols_idx, NC_COLS = mkcols(cols_l0a, lambda c: {"l0a_ind": l0a_inds[c]})
    p = P()
    xin = p.dram_in("xin", [D, TOK], F32)
    xout = p.dram_out("xout", [D, TOK], F32)
    fqi = p.dram_in("fq", [D, TOK], BF16)
    fki = p.dram_in("fk", [D, TOK], BF16)
    fvi = p.dram_in("fv", [D, TOK], BF16)
    allst = p.dram_in("l0a_allstates", [2 * NBLK * 128, 8 * 65], F32)
    C = _mk_ctx(p, 0, cols_idx, consts_idx, NC_COLS, NC_CONSTS)
    emit_load_stream(p, C, xin)
    emit_l0attna_passB(p, C, fqi, fki, fvi, allst)
    emit_store_stream(p, C, xout)
    nc2 = p.build()
    maps = base_maps()
    for c in range(NCORES):
        maps[c].update({"xin": xs[c], "fq": r1[c]["fq"], "fk": r1[c]["fk"],
                        "fv": r1[c]["fv"], "l0a_allstates": allstates, **all_w})
    r2 = run(nc2, filt(nc2, maps))
    xs = [r2[c]["xout"] for c in range(NCORES)]

    cols_blobs, cols_idx, NC_COLS = mkcols({})
    moe_cols = cols_blobs
    p = P()
    xin = p.dram_in("xin", [D, TOK], F32)
    xout = p.dram_out("xout", [D, TOK], F32)
    C = _mk_ctx(p, 0, cols_idx, consts_idx, NC_COLS, NC_CONSTS)
    emit_load_stream(p, C, xin)
    emit_l0ffn(p, C)
    emit_store_stream(p, C, xout)
    nc_moe = p.build()

    def run_moe():
        maps = [{"cols": moe_cols[c], "consts": consts_blob} for c in range(NCORES)]
        for c in range(NCORES):
            maps[c].update({"xin": xs[c], **all_w})
        return [r["xout"] for r in run(nc_moe, filt(nc_moe, maps))]

    xs = run_moe()

    cols_blobs, cols_idx, NC_COLS = mkcols(cols_l0b)
    p = P()
    xin = p.dram_in("xin", [D, TOK], F32)
    xout = p.dram_out("xout", [D, TOK], F32)
    C = _mk_ctx(p, 0, cols_idx, consts_idx, NC_COLS, NC_CONSTS)
    emit_load_stream(p, C, xin)
    emit_l0attnb(p, C)
    emit_store_stream(p, C, xout)
    nc4 = p.build()
    maps = base_maps()
    for c in range(NCORES):
        maps[c].update({"xin": xs[c], **all_w})
    xs = [r["xout"] for r in run(nc4, filt(nc4, maps))]

    xs = run_moe()

    cols_blobs, cols_idx, NC_COLS = mkcols(cols_l1a)
    p = P()
    xin = p.dram_in("xin", [D, TOK], F32)
    qd = p.dram_out("qd", [D, TOK], BF16)
    kd = p.dram_out("kd", [D, TOK], BF16)
    vd = p.dram_out("vd", [TOK, D], BF16)
    rowsd = p.dram_out("rowsd", [5, TOK], F32)
    momd = p.dram_out("momd", [1, 24], F32)
    C = _mk_ctx(p, 0, cols_idx, consts_idx, NC_COLS, NC_CONSTS)
    emit_load_stream(p, C, xin)
    emit_l1attna_passA(p, C, qd, kd, vd, rowsd, momd)
    nc6 = p.build()
    maps = base_maps()
    for c in range(NCORES):
        maps[c].update({"xin": xs[c], **all_w})
    r6 = run(nc6, filt(nc6, maps))

    k_all = np.zeros((2, D, T), np.dtype(r6[0]["kd"].dtype))
    v_all = np.zeros((2, T, D), np.dtype(r6[0]["vd"].dtype))
    for c in range(NCORES):
        for i, (bt, blk) in enumerate(inst_map(c)):
            sl = slice(blk * BLK, (blk + 1) * BLK)
            k_all[bt, :, sl] = r6[c]["kd"][:, i * BLK:(i + 1) * BLK]
            v_all[bt, sl, :] = r6[c]["vd"][i * BLK:(i + 1) * BLK, :]
    krows, qrows = host_l1a_rows([r6[c]["rowsd"] for c in range(NCORES)])
    mom_all = np.ascontiguousarray(np.concatenate([r6[c]["momd"] for c in range(NCORES)], axis=0))

    p = P()
    xin = p.dram_in("xin", [D, TOK], F32)
    xout = p.dram_out("xout", [D, TOK], F32)
    qdi = p.dram_in("qd", [D, TOK], BF16)
    momi = p.dram_in("l1a_momall", [8, 24], F32)
    kalli = p.dram_in("l1a_kall", [2, D, T], BF16)
    valli = p.dram_in("l1a_vall", [2, T, D], BF16)
    C = _mk_ctx(p, 0, cols_idx, consts_idx, NC_COLS, NC_CONSTS)
    emit_load_stream(p, C, xin)
    emit_l1attna_passB(p, C, qdi, momi, kalli, valli)
    emit_store_stream(p, C, xout)
    nc7 = p.build()
    maps = base_maps()
    for c in range(NCORES):
        maps[c].update({"xin": xs[c], "qd": r6[c]["qd"], "l1a_momall": mom_all,
                        "l1a_kall": k_all, "l1a_vall": v_all,
                        "l1a_qrows": qrows[c], "l1a_krows": krows, **all_w})
    xs = [r["xout"] for r in run(nc7, filt(nc7, maps))]

    cols_blobs, cols_idx, NC_COLS = mkcols(cols_l1f)
    l1f_cols = cols_blobs
    p = P()
    xin = p.dram_in("xin", [D, TOK], F32)
    xout = p.dram_out("xout", [D, TOK], F32)
    C = _mk_ctx(p, 0, cols_idx, consts_idx, NC_COLS, NC_CONSTS)
    emit_load_stream(p, C, xin)
    MT = build_l1ffn_weight(p, C)
    emit_l1ffn(p, C, MT)
    emit_store_stream(p, C, xout)
    nc8 = p.build()

    def run_l1f():
        maps = [{"cols": l1f_cols[c], "consts": consts_blob} for c in range(NCORES)]
        for c in range(NCORES):
            maps[c].update({"xin": xs[c], **all_w})
        return [r["xout"] for r in run(nc8, filt(nc8, maps))]

    xs = run_l1f()

    cols_blobs, cols_idx, NC_COLS = mkcols(cols_mb, host_mamba_ind)
    halos = []
    xfull = host_unshard_x(xs)
    for c in range(NCORES):
        hl = np.zeros((D, 24), np.float32)
        for i, (bt, blk) in enumerate(inst_map(c)):
            for j in range(3):
                t0 = blk * BLK - 3 + j
                if t0 >= 0:
                    hl[:, i * 6 + j] = xfull[bt, t0]
                t1 = (blk + 1) * BLK + j
                if t1 < T:
                    hl[:, i * 6 + 3 + j] = xfull[bt, t1]
        halos.append(hl)

    p = P()
    xin = p.dram_in("xin", [D, TOK], F32)
    halod = p.dram_in("mb_halo", [D, 24], F32)
    exports = {}
    for d in ("F", "B"):
        exports[d] = (p.dram_out(f"yd{d}", [2048, TOK], BF16),
                      p.dram_out(f"ccd{d}", [16, TOK], BF16),
                      p.dram_out(f"etd{d}", [TOK, 32], F32),
                      p.dram_out(f"eed{d}", [16, 32], F32),
                      p.dram_out(f"stated{d}", [4 * 16, 2048], F32),
                      p.dram_out(f"decd{d}", [4, 32], F32))
    C = _mk_ctx(p, 0, cols_idx, consts_idx, NC_COLS, NC_CONSTS)
    emit_load_stream(p, C, xin)
    emit_l1attnb_A(p, C, halod, exports)
    nc9 = p.build()
    maps = base_maps()
    for c in range(NCORES):
        maps[c].update({"xin": xs[c], "mb_halo": halos[c], **all_w})
    r9 = run(nc9, filt(nc9, maps))

    states_g = {d: np.zeros((2, 16, 16, 2048), np.float32) for d in ("F", "B")}
    dec_g = {d: np.ones((2, 16, 32), np.float32) for d in ("F", "B")}
    for c in range(NCORES):
        for i, (bt, blk) in enumerate(inst_map(c)):
            for d in ("F", "B"):
                states_g[d][bt, blk] = r9[c][f"stated{d}"][i * 16:(i + 1) * 16]
                dec_g[d][bt, blk] = r9[c][f"decd{d}"][i]

    def exch(core):
        out = {}
        for d in ("F", "B"):
            order = list(range(16)) if d == "F" else list(range(15, -1, -1))
            allstm = np.zeros((16 * 128, 1024), np.float32)
            drt = np.zeros((128, 16), np.float32)
            for bi, blkq in enumerate(order):
                for i, (bt, blk) in enumerate(inst_map(core)):
                    S = states_g[d][bt, blkq].reshape(16, 32, 64).transpose(1, 0, 2).reshape(32, 1024)
                    allstm[bi * 128 + i * 32: bi * 128 + (i + 1) * 32] = S
                    drt[i * 32:(i + 1) * 32, bi] = dec_g[d][bt, blkq]
            out[f"mb{d}_allst"] = allstm
            out[f"mb{d}_decrepT"] = drt
        return out

    p = P()
    xin = p.dram_in("xin", [D, TOK], F32)
    xout = p.dram_out("xout", [D, TOK], F32)
    imports = {}
    for d in ("F", "B"):
        imports[d] = (p.dram_in(f"yd{d}", [2048, TOK], BF16),
                      p.dram_in(f"ccd{d}", [16, TOK], BF16),
                      p.dram_in(f"etd{d}", [TOK, 32], F32),
                      p.dram_in(f"eed{d}", [16, 32], F32))
    hsc = p.dram_tmp("haccsc", [128, 1024], F32)
    C = _mk_ctx(p, 0, cols_idx, consts_idx, NC_COLS, NC_CONSTS)
    emit_load_stream(p, C, xin)
    emit_l1attnb_B(p, C, imports, hsc)
    emit_store_stream(p, C, xout)
    nc10 = p.build()
    maps = base_maps()
    for c in range(NCORES):
        m = {"xin": xs[c], **all_w, **exch(c)}
        for d in ("F", "B"):
            for nm in ("yd", "ccd", "etd", "eed"):
                m[f"{nm}{d}"] = r9[c][f"{nm}{d}"]
        maps[c].update(m)
    xs = [r["xout"] for r in run(nc10, filt(nc10, maps))]

    xs = run_l1f()

    return host_unshard_x(xs).astype(np.float32)
